# version 06
# baseline (speedup 1.0000x reference)
import math
import os
from contextlib import ExitStack

KSER = int(os.environ.get("KSER", "0"))

import numpy as np
import ml_dtypes

import concourse.bass as bass
import concourse.mybir as mybir
from concourse.bass_utils import run_bass_kernel_spmd

F32 = mybir.dt.float32
BF16 = mybir.dt.bfloat16
AF = mybir.ActivationFunctionType
ALU = mybir.AluOpType
AX = mybir.AxisListType

S, D, TT = 2048, 1024, 128
NT = S // TT
NC = D // 128
NH, DH, DHR, DL = 16, 64, 32, 256
NE, DHID = 8, 2048
INV_SQRT_DQK = 1.0 / math.sqrt(96.0)
NEG = -1.0e30

ALL_PARTS = ("moe1", "mamba", "moe2", "ffn1", "attn", "ffn2")

_UID = [0]


def _nm(s):
    _UID[0] += 1
    return f"{s}_{_UID[0]}"


class Op:
    __slots__ = ("fn", "deps", "dma", "dma_idx", "need_inc", "mval")

    def __init__(self, fn, dma, dma_idx):
        self.fn = fn
        self.deps = set()
        self.dma = dma
        self.dma_idx = dma_idx
        self.need_inc = False
        self.mval = 0


class Prog:
    CENG = ("tensor", "vector", "scalar", "gpsimd")
    DSEM = {"sync": "dsync", "gpsimd": "dgp", "scalar": "dscal", "vector": "dvec"}
    QUEUES = ("sync", "tensor", "vector", "scalar", "gpsimd")

    def __init__(self, nc, sems):
        self.nc = nc
        self.sems = sems
        self.ops = {k: [] for k in self.QUEUES}
        self.ndma = {k: 0 for k in self.QUEUES}
        self.res = {}
        self.addr = {}
        self.rowel = {}

    def _keys(self, ap):
        t = ap.tensor
        nm = t.name
        if nm not in self.addr:
            return [("D", nm)]
        esz = mybir.dt.size(ap.dtype)
        row = self.rowel[nm]
        off = ap.offset % row
        ext = 0
        for s_, n_ in list(ap.ap)[1:]:
            if n_ > 1 and s_ > 0:
                ext += (n_ - 1) * s_
        lo = self.addr[nm] + off * esz
        hi = lo + ext * esz + esz
        if getattr(t, "space", None) is not None and "PSUM" in str(t.space).upper():
            return [("P", b) for b in range(lo // 2048, (hi - 1) // 2048 + 1)]
        return [("S", g) for g in range(lo // 512, (hi - 1) // 512 + 1)]

    def emit(self, eng, fn, reads=(), writes=(), dma=False):
        ops = self.ops[eng]
        idx = len(ops)
        if dma:
            op = Op(fn, True, self.ndma[eng])
            self.ndma[eng] += 1
        else:
            op = Op(fn, False, 0)
        tok = (eng, idx)
        if KSER:
            for q in self.QUEUES:
                if self.ops[q] and not (q == eng):
                    op.deps.add((q, len(self.ops[q]) - 1))
            if idx:
                op.deps.add((eng, idx - 1))

        def add(t_):
            if t_ is not None and t_ != tok:
                op.deps.add(t_)

        rkeys = [k for ap in reads if ap is not None for k in self._keys(ap)]
        wkeys = [k for ap in writes if ap is not None for k in self._keys(ap)]
        for k in rkeys:
            e = self.res.get(k)
            if e:
                add(e["w"])
        for k in wkeys:
            e = self.res.get(k)
            if e:
                add(e["w"])
                for tkn in e["r"]:
                    add(tkn)
        ops.append(op)
        for k in rkeys:
            self.res.setdefault(k, {"w": None, "r": []})["r"].append(tok)
        for k in wkeys:
            self.res[k] = {"w": tok, "r": [tok]} if k in rkeys else {"w": tok, "r": []}
        return tok

    def _wait_needed(self, cq, pq, pop):
        if pop.dma:
            return True
        if KSER:
            return True
        if pq == cq and pq == "tensor":
            return False
        return True

    def flush(self, final=False):
        if not final:
            return
        ops = self.ops
        for cq in self.QUEUES:
            for cop in ops[cq]:
                for (pq, pi) in cop.deps:
                    pop = ops[pq][pi]
                    if not pop.dma and self._wait_needed(cq, pq, pop):
                        pop.need_inc = True
        for q in self.QUEUES:
            for op_ in reversed(ops[q]):
                if not op_.dma:
                    op_.need_inc = True
                    break
        for q in self.QUEUES:
            m = 0
            for op_ in ops[q]:
                if (not op_.dma) and op_.need_inc:
                    m += 1
                op_.mval = m
        sched = {q: [] for q in self.QUEUES}
        last_wait = {}
        for q in self.QUEUES:
            for op_ in ops[q]:
                deps = {}
                for (pq, pi) in op_.deps:
                    pop = ops[pq][pi]
                    if not self._wait_needed(q, pq, pop):
                        continue
                    if pop.dma:
                        sn, v = self.DSEM[pq], 16 * (pop.dma_idx + 1)
                    else:
                        sn, v = pq, pop.mval
                        assert pop.need_inc and v > 0
                    if deps.get(sn, 0) < v:
                        deps[sn] = v
                waits = []
                for sn, v in sorted(deps.items()):
                    lw = last_wait.get((q, sn), 0)
                    if v <= lw:
                        continue
                    waits.append((self.sems[sn], v))
                    last_wait[(q, sn)] = v
                if op_.dma:
                    inc = (self.sems[self.DSEM[q]], 16)
                elif op_.need_inc:
                    inc = (self.sems[q], 1)
                else:
                    inc = None
                sched[q].append((waits, op_.fn, inc))
        fwaits = []
        for q in self.QUEUES:
            if ops[q]:
                mv = ops[q][-1].mval
                if mv:
                    fwaits.append((self.sems[q], mv))
            if self.ndma[q]:
                fwaits.append((self.sems[self.DSEM[q]], 16 * self.ndma[q]))
        sched["sync"].append((fwaits, None, None))

        def runner(lst):
            def run(e):
                for waits, fn, inc in lst:
                    for sh, v in waits:
                        e.wait_ge(sh, v)
                    if fn is not None:
                        if inc is not None:
                            fn(e).then_inc(inc[0], inc[1])
                        else:
                            fn(e)
            return run

        with self.nc.Block() as block:
            for eng in self.QUEUES:
                if sched[eng]:
                    getattr(block, eng)(runner(sched[eng]))

    @staticmethod
    def _isap(x):
        return hasattr(x, "tensor")

    def dma(self, out, in_, eng="sync"):
        self.emit(eng, lambda e: e.dma_start(out, in_),
                  reads=[in_], writes=[out], dma=True)

    def mm(self, out, lhsT, rhs, start=True, stop=True):
        rd = [lhsT, rhs] + ([out] if not start else [])
        self.emit("tensor",
                  lambda e: e.matmul(out, lhsT, rhs, start=start, stop=stop),
                  reads=rd, writes=[out])

    def tr(self, out, in_, ident):
        self.emit("tensor", lambda e: e.transpose(out, in_, ident),
                  reads=[in_, ident], writes=[out])

    def tt(self, out, a, b, op=ALU.mult, eng="vector"):
        self.emit(eng, lambda e: e.tensor_tensor(out=out, in0=a, in1=b, op=op),
                  reads=[a, b], writes=[out])

    def ts(self, out, in0, s1, op0, s2=None, op1=ALU.mult, accum_out=None, eng="vector"):
        rd = [in0] + [x for x in (s1, s2) if self._isap(x)]
        wr = [out] + ([accum_out] if accum_out is not None else [])
        if s2 is None and accum_out is None:
            self.emit(eng, lambda e: e.tensor_scalar(
                out=out, in0=in0, scalar1=s1, scalar2=None, op0=op0),
                reads=rd, writes=wr)
        else:
            self.emit(eng, lambda e: e.tensor_scalar(
                out=out, in0=in0, scalar1=s1, scalar2=s2, op0=op0, op1=op1,
                accum_out=accum_out), reads=rd, writes=wr)

    def stt(self, out, in0, scalar, in1, op0=ALU.mult, op1=ALU.add, eng="vector"):
        rd = [in0, in1] + ([scalar] if self._isap(scalar) else [])
        self.emit(eng, lambda e: e.scalar_tensor_tensor(
            out=out, in0=in0, scalar=scalar, in1=in1, op0=op0, op1=op1),
            reads=rd, writes=[out])

    def act(self, out, in_, func, bias=0.0, scale=1.0, accum_out=None):
        rd = [in_] + [x for x in (bias, scale) if self._isap(x)]
        wr = [out] + ([accum_out] if accum_out is not None else [])
        self.emit("scalar", lambda e: e.activation(
            out=out, in_=in_, func=func, bias=bias, scale=scale,
            accum_out=accum_out), reads=rd, writes=wr)

    def red(self, out, in_, op=ALU.add):
        self.emit("vector",
                  lambda e: e.tensor_reduce(out=out, in_=in_, axis=AX.X, op=op),
                  reads=[in_], writes=[out])

    def scan(self, out, d0, d1, initial):
        self.emit("vector", lambda e: e.tensor_tensor_scan(
            out=out, data0=d0, data1=d1, initial=initial,
            op0=ALU.mult, op1=ALU.add), reads=[d0, d1, initial], writes=[out])

    def copy(self, out, in_, eng="vector"):
        self.emit(eng, lambda e: e.tensor_copy(out, in_),
                  reads=[in_], writes=[out])

    def memset(self, ap, v, eng="vector"):
        self.emit(eng, lambda e: e.memset(ap, v), writes=[ap])

    def recip(self, out, in_, eng="vector"):
        self.emit(eng, lambda e: e.reciprocal(out, in_),
                  reads=[in_], writes=[out])

    def sb(self, ex, name, shape, dtype):
        base = self.nc.sbuf_base
        h = ex.enter_context(self.nc.sbuf_tensor(_nm(name), shape, dtype))
        self.addr[h.name] = base
        self.rowel[h.name] = int(np.prod(shape[1:])) if len(shape) > 1 else 1
        return h

    def ps(self, ex, name, shape, dtype):
        base = self.nc.psum_base * 2048
        h = ex.enter_context(self.nc.psum_tensor(_nm(name), shape, dtype))
        self.addr[h.name] = base
        self.rowel[h.name] = int(np.prod(shape[1:])) if len(shape) > 1 else 1
        return h


def sl(t, w):
    return slice(t * w, (t + 1) * w)


def ln_tile(p, c, t, zhatT_dest, half_res=False, f32_scr=False):
    xs = c.x[:, sl(t, D)]
    p.red(c.sumc[:], xs)
    p.act(c.scr[:], xs, AF.Square, accum_out=c.sqc[:])
    p.ts(c.mc[:], c.sumc[:], 1.0 / D, ALU.mult)
    p.ts(c.msq[:], c.mc[:], c.mc[:], ALU.mult)
    p.stt(c.varc[:], c.sqc[:], 1.0 / D, c.msq[:], ALU.mult, ALU.subtract)
    p.act(c.lnc[:], c.varc[:], AF.Ln, bias=c.epscol[:])
    p.act(c.rstdc[:], c.lnc[:], AF.Exp, scale=-0.5)
    p.ts(c.zhat[:], xs, c.mc[:], ALU.subtract, s2=c.rstdc[:], op1=ALU.mult)
    if f32_scr:
        p.ts(c.scr[:], xs, c.mc[:], ALU.subtract, s2=c.rstdc[:], op1=ALU.mult)
    if half_res:
        p.stt(xs, c.zhat[:], 0.5, xs, ALU.mult, ALU.add)
    for ci in range(NC):
        p.tr(c.pm_trb[:, sl(ci, 128)], c.zhat[:, sl(ci, 128)], c.ident_b[:])
    p.act(zhatT_dest, c.pm_trb[:], AF.Copy)


def emit_moe(p, c, st):
    nc = p.nc
    with ExitStack() as ex:
        zT = p.sb(ex, "moe_zT", [128, NC * S], BF16)
        gate = p.sb(ex, "moe_gate", [128, NC * 8], F32)
        cw = p.sb(ex, "moe_cw", [128, NT * 8], F32)
        g8 = p.sb(ex, "moe_g8", [128, 8], F32)
        gm = p.sb(ex, "moe_gm", [128, 6], F32)
        w1 = p.sb(ex, "moe_w1", [128, NC * DHID], BF16)
        w2 = p.sb(ex, "moe_w2", [128, 16 * D], BF16)
        h = p.sb(ex, "moe_h", [128, 16 * 512], BF16)
        b1n = p.sb(ex, "moe_b1n", [16, 128], F32)
        b1c = p.sb(ex, "moe_b1c", [128, 16], F32)
        b2s = p.sb(ex, "moe_b2s", [8, D], F32)
        cwT = p.sb(ex, "moe_cwT", [8, 128], F32)
        zTf = p.sb(ex, "moe_zTf", [128, D], F32)
        pm = p.ps(ex, "moe_pm", [128, 512], F32)
        pmo = p.ps(ex, "moe_pmo", [128, 512], F32)
        pms = p.ps(ex, "moe_pms", [128, 128], F32)
        pmtf = p.ps(ex, "moe_pmtf", [128, D], F32)

        p.dma(gate.rearrange("p (cc e) -> p cc e", e=8),
              st.gate_W.rearrange("(cc p) e -> p cc e", p=128))
        p.dma(b2s[:], st.ffn_b2)

        for t in range(NT):
            ln_tile(p, c, t,
                    bass.AP(tensor=zT, offset=t * TT,
                            ap=[[NC * S, 128], [S, NC], [1, TT]]),
                    half_res=True, f32_scr=True)
            for ci in range(NC):
                p.tr(pmtf[:, sl(ci, 128)], c.scr[:, sl(ci, 128)], c.ident_f[:])
            p.act(zTf[:], pmtf[:], AF.Copy)
            for fc in range(NC):
                p.mm(pms[:, 0:8], zTf[:, sl(fc, 128)],
                     gate[:, sl(fc, 8)], start=(fc == 0), stop=(fc == NC - 1))
            p.red(gm[:, 0:1], pms[:, 0:8], op=ALU.max)
            p.ts(g8[:], pms[:, 0:8], gm[:, 0:1], ALU.subtract)
            p.act(g8[:], g8[:], AF.Exp)
            p.red(gm[:, 1:2], g8[:])
            p.recip(gm[:, 2:3], gm[:, 1:2])
            p.ts(g8[:], g8[:], gm[:, 2:3], ALU.mult)
            p.red(gm[:, 0:1], g8[:], op=ALU.max)
            p.ts(c.scr[:, 0:8], g8[:], gm[:, 0:1], ALU.is_ge)
            p.stt(c.scr[:, 8:16], c.scr[:, 0:8], NEG, g8[:], ALU.mult, ALU.add)
            p.red(gm[:, 1:2], c.scr[:, 8:16], op=ALU.max)
            p.tt(gm[:, 3:4], gm[:, 0:1], gm[:, 1:2], op=ALU.add)
            p.ts(gm[:, 3:4], gm[:, 3:4], 1.0e-8, ALU.add)
            p.recip(gm[:, 4:5], gm[:, 3:4])
            p.ts(c.scr[:, 0:8], g8[:], gm[:, 1:2], ALU.is_ge)
            p.tt(c.scr[:, 8:16], g8[:], c.scr[:, 0:8], op=ALU.mult)
            p.ts(cw[:, sl(t, 8)], c.scr[:, 8:16], gm[:, 4:5], ALU.mult,
                 s2=0.5, op1=ALU.mult)

        for e_i in range(NE):
            p.dma(w1.rearrange("p (cc hh) -> p cc hh", cc=NC),
                  st.ffn_W1[e_i].rearrange("(cc p) hh -> p cc hh", p=128),
                  eng="gpsimd")
            p.dma(w2.rearrange("p (hc o) -> p hc o", o=D),
                  st.ffn_W2[e_i].rearrange("(hc p) o -> p hc o", p=128),
                  eng="gpsimd")
            p.dma(b1n[:], st.ffn_b1[e_i].rearrange("(a b) -> a b", a=16))
            p.tr(pms[0:128, 0:16], b1n[:], c.ident_f[0:16, 0:16])
            p.act(b1c[:], pms[:, 0:16], AF.Copy)
            for stile in range(4):
                for hc in range(16):
                    for fc in range(NC):
                        p.mm(pm[:], w1[:, fc * DHID + hc * 128:fc * DHID + (hc + 1) * 128],
                             zT[:, fc * S + stile * 512:fc * S + (stile + 1) * 512],
                             start=(fc == 0), stop=(fc == NC - 1))
                    p.act(h[:, sl(hc, 512)], pm[:], AF.Relu, bias=b1c[:, hc:hc + 1])
                for tk in range(4):
                    t = stile * 4 + tk
                    for half in range(2):
                        for hc in range(16):
                            p.mm(pmo[:],
                                 h[:, hc * 512 + tk * 128:hc * 512 + (tk + 1) * 128],
                                 w2[:, hc * D + half * 512:hc * D + (half + 1) * 512],
                                 start=(hc == 0), stop=(hc == 15))
                        xsl = c.x[:, t * D + half * 512:t * D + (half + 1) * 512]
                        p.stt(xsl, pmo[:], cw[:, t * 8 + e_i:t * 8 + e_i + 1], xsl)

        for t in range(NT):
            p.tr(pms[0:8, 0:128], cw[:, sl(t, 8)], c.ident_f[:])
            p.act(cwT[:], pms[0:8, 0:128], AF.Copy)
            for half in range(2):
                p.mm(pmo[:], cwT[:], b2s[:, sl(half, 512)])
                xsl = c.x[:, t * D + half * 512:t * D + (half + 1) * 512]
                p.tt(xsl, xsl, pmo[:], op=ALU.add)
        p.flush()


def emit_mamba(p, c, st):
    nc = p.nc
    with ExitStack() as ex:
        zhT = [p.sb(ex, "mb_zhT", [128, D], BF16) for _ in range(2)]
        wbt = p.sb(ex, "mb_wbt", [128, NC * 16], BF16)
        wct = p.sb(ex, "mb_wct", [128, NC * 16], BF16)
        wdt = p.sb(ex, "mb_wdt", [128, NC], BF16)
        wnat = p.sb(ex, "mb_wnat", [16, D], F32)
        adl = p.sb(ex, "mb_adl", [128, 128], F32)
        iadl = p.sb(ex, "mb_iadl", [128, 128], F32)
        state = p.sb(ex, "mb_state", [128, 128], F32)
        bmt = p.sb(ex, "mb_bmt", [128, 16], F32)
        cmt = p.sb(ex, "mb_cmt", [128, 16], F32)
        bbb = p.sb(ex, "mb_bbb", [128, 16], F32)
        bcb = p.sb(ex, "mb_bcb", [128, 16], F32)
        bdc = p.sb(ex, "mb_bdc", [128, 1], F32)
        brow = p.sb(ex, "mb_brow", [1, 16], F32)
        dsp = p.sb(ex, "mb_dsp", [128, 6], F32)
        drow = p.sb(ex, "mb_drow", [1, 128], F32)
        dcols = [p.sb(ex, "mb_dcols", [128, 128], F32) for _ in range(2)]
        flat = [p.sb(ex, "mb_flat", [1, 2048], F32) for _ in range(2)]
        bmb = [p.sb(ex, "mb_bmb", [128, 2048], F32) for _ in range(2)]
        cmb = [p.sb(ex, "mb_cmb", [128, 2048], F32) for _ in range(2)]
        abuf = [p.sb(ex, "mb_abuf", [128, 2048], F32) for _ in range(2)]
        ab = [p.sb(ex, "mb_ab", [128, 2048], F32) for _ in range(2)]
        ww = [p.sb(ex, "mb_ww", [128, 2048], F32) for _ in range(2)]
        bw = [p.sb(ex, "mb_bw", [128, 2048], F32) for _ in range(1)]
        sh = [p.sb(ex, "mb_sh", [128, 2048], F32) for _ in range(2)]
        ybt = p.sb(ex, "mb_ybt", [128, 128], F32)
        pm = p.ps(ex, "mb_pm", [128, 512], F32)
        pms = p.ps(ex, "mb_pms", [128, 128], F32)
        pmy = p.ps(ex, "mb_pmy", [128, 128], F32)

        p.dma(adl.rearrange("p (cc n) -> p cc n", n=16),
              st.A.rearrange("(cc p) n -> p cc n", p=128))
        p.recip(iadl[:], adl[:])
        p.memset(state[:], 0.0)
        p.dma(wnat[:], st.W_B)
        for ci in range(NC):
            p.tr(pms[:, 0:16], wnat[:, sl(ci, 128)], c.ident_f[0:16, 0:16])
            p.act(wbt[:, sl(ci, 16)], pms[:, 0:16], AF.Copy)
        p.dma(wnat[:], st.W_C)
        for ci in range(NC):
            p.tr(pms[:, 0:16], wnat[:, sl(ci, 128)], c.ident_f[0:16, 0:16])
            p.act(wct[:, sl(ci, 16)], pms[:, 0:16], AF.Copy)
        p.dma(wnat[0:1, :], st.W_Delta)
        for ci in range(NC):
            p.tr(pms[:, 0:1], wnat[0:1, sl(ci, 128)], c.ident_f[0:1, 0:1])
            p.act(wdt[:, ci:ci + 1], pms[:, 0:1], AF.Copy)
        p.dma(brow[:], st.b_B.rearrange("(a n) -> a n", a=1))
        p.mm(pms[:, 0:16], c.ones_f[0:1, :], brow[:])
        p.act(bbb[:], pms[:, 0:16], AF.Copy)
        p.dma(brow[:], st.b_C.rearrange("(a n) -> a n", a=1))
        p.mm(pms[:, 0:16], c.ones_f[0:1, :], brow[:])
        p.act(bcb[:], pms[:, 0:16], AF.Copy)
        p.dma(brow[0:1, 0:1], st.b_Delta.rearrange("(a n) -> a n", a=1))
        p.dma(brow[0:1, 1:2], st.bias_Delta.rearrange("(a n) -> a n", a=1))
        p.tt(brow[0:1, 2:3], brow[0:1, 0:1], brow[0:1, 1:2], op=ALU.add)
        p.mm(pms[:, 0:1], c.ones_f[0:1, :], brow[0:1, 2:3])
        p.act(bdc[:], pms[:, 0:1], AF.Copy)

        def prep_tile(t):
            tp = t % 2
            ln_tile(p, c, t,
                    bass.AP(tensor=zhT[tp], offset=0,
                            ap=[[D, 128], [128, NC], [1, TT]]))
            for fc in range(NC):
                p.mm(pms[:, 0:16], zhT[tp][:, sl(fc, 128)], wbt[:, sl(fc, 16)],
                     start=(fc == 0), stop=(fc == NC - 1))
            p.tt(bmt[:], pms[:, 0:16], bbb[:], op=ALU.add)
            for fc in range(NC):
                p.mm(pms[:, 0:16], zhT[tp][:, sl(fc, 128)], wct[:, sl(fc, 16)],
                     start=(fc == 0), stop=(fc == NC - 1))
            p.tt(cmt[:], pms[:, 0:16], bcb[:], op=ALU.add)
            for fc in range(NC):
                p.mm(pms[:, 0:1], zhT[tp][:, sl(fc, 128)], wdt[:, fc:fc + 1],
                     start=(fc == 0), stop=(fc == NC - 1))
            p.ts(dsp[:, 0:1], pms[:, 0:1], bdc[:], ALU.add)
            p.act(dsp[:, 1:2], dsp[:, 0:1], AF.Abs)
            p.act(dsp[:, 2:3], dsp[:, 1:2], AF.Exp, scale=-1.0)
            p.act(dsp[:, 3:4], dsp[:, 2:3], AF.Ln, bias=c.onecol[:])
            p.act(dsp[:, 4:5], dsp[:, 0:1], AF.Relu)
            p.tt(dsp[:, 5:6], dsp[:, 3:4], dsp[:, 4:5], op=ALU.add)
            p.tr(pms[0:1, 0:128], dsp[:, 5:6], c.ident_f[:])
            p.act(drow[:], pms[0:1, 0:128], AF.Copy)
            p.mm(pms[:], c.ones_f[0:1, :], drow[:])
            p.act(dcols[tp][:], pms[:], AF.Copy)
            bmflat = bass.AP(tensor=st.bm_scr[tp].tensor, offset=0,
                             ap=[[2048, 1], [1, 2048]])
            cmflat = bass.AP(tensor=st.cm_scr[tp].tensor, offset=0,
                             ap=[[2048, 1], [1, 2048]])
            p.dma(st.bm_scr[tp], bmt[:])
            p.dma(flat[tp][:], bmflat)
            for k in range(4):
                p.mm(pm[:], c.ones_f[0:1, :], flat[tp][:, sl(k, 512)])
                p.act(bmb[tp][:, sl(k, 512)], pm[:], AF.Copy)
            p.dma(st.cm_scr[tp], cmt[:])
            p.dma(flat[tp][:], cmflat)
            for k in range(4):
                p.mm(pm[:], c.ones_f[0:1, :], flat[tp][:, sl(k, 512)])
                p.act(cmb[tp][:, sl(k, 512)], pm[:], AF.Copy)

        def unit_head(t, ci):
            tp, b = t % 2, ci % 2
            v3 = [128, TT, 16]
            p.tt(abuf[b].rearrange("p (t n) -> p t n", n=16),
                 dcols[tp][:, :, None].broadcast_to(v3),
                 adl[:, sl(ci, 16)][:, None, :].broadcast_to(v3), eng="gpsimd")
            p.tt(ww[b].rearrange("p (t n) -> p t n", n=16),
                 bmb[tp].rearrange("p (t n) -> p t n", n=16),
                 zhT[tp][:, sl(ci, 128)][:, :, None].broadcast_to(v3),
                 eng="gpsimd")
            p.act(ab[b][:], abuf[b][:], AF.Exp)
            p.stt(bw[0].rearrange("p (t n) -> p t n", n=16),
                  ab[b].rearrange("p (t n) -> p t n", n=16), -1.0,
                  iadl[:, sl(ci, 16)][:, None, :].broadcast_to(v3),
                  ALU.add, ALU.mult)
            p.tt(bw[0][:], bw[0][:], ww[b][:])
            for n in range(16):
                p.scan(bass.AP(tensor=sh[b], offset=n, ap=[[2048, 128], [16, TT]]),
                       bass.AP(tensor=ab[b], offset=n, ap=[[2048, 128], [16, TT]]),
                       bass.AP(tensor=bw[0], offset=n, ap=[[2048, 128], [16, TT]]),
                       state[:, ci * 16 + n:ci * 16 + n + 1])
            p.copy(state[:, sl(ci, 16)], sh[b][:, 2032:2048])

        def unit_tail(t, ci):
            tp, b = t % 2, ci % 2
            p.tt(abuf[b][:], sh[b][:], cmb[tp][:], eng="gpsimd")
            p.red(ybt[:], abuf[b].rearrange("p (t n) -> p t n", n=16))
            p.tr(pmy[:], ybt[:], c.ident_f[:])
            xsl = c.x[:, t * D + ci * 128:t * D + (ci + 1) * 128]
            p.tt(xsl, xsl, pmy[:], op=ALU.add)

        units = [(t, ci) for t in range(NT) for ci in range(NC)]
        prep_tile(0)
        for i, (t, ci) in enumerate(units):
            if ci == 0 and t + 1 < NT:
                prep_tile(t + 1)
            unit_head(t, ci)
            if i >= 1:
                unit_tail(*units[i - 1])
        unit_tail(*units[-1])
        p.flush()


def emit_l1ffn_prep(p, c, st):
    nc = p.nc
    with ExitStack() as ex:
        ldn = p.sb(ex, "lp_ldn", [4, 100], F32)
        ldT = p.sb(ex, "lp_ldT", [100, 4], F32)
        dax = p.sb(ex, "lp_dax", [100, D], F32)
        lun = p.sb(ex, "lp_lun", [50, 4], F32)
        luT = p.sb(ex, "lp_luT", [4, 50], F32)
        m1 = p.sb(ex, "lp_m1", [4, D], F32)
        m2 = p.sb(ex, "lp_m2", [50, D], F32)
        upn = p.sb(ex, "lp_upn", [128, NC * 50], F32)
        upT = p.sb(ex, "lp_upT", [50, D], F32)
        wsb = p.sb(ex, "lp_wsb", [128, NC * D], F32)
        wet = p.sb(ex, "lp_wet", [128, NC * D], BF16)
        wtt = p.sb(ex, "lp_wtt", [128, 128], F32)
        brw = p.sb(ex, "lp_brw", [1, D], F32)
        hbs = p.sb(ex, "lp_hbs", [128, D], F32)
        pm = p.ps(ex, "lp_pm", [128, 512], F32)
        pmt = p.ps(ex, "lp_pmt", [128, 128], F32)

        p.dma(ldn[:], st.lora_down)
        p.dma(dax[:], st.down_aux)
        p.dma(lun[:], st.lora_up)
        p.dma(upn.rearrange("p (cc u) -> p cc u", u=50),
              st.up_aux.rearrange("(cc p) u -> p cc u", p=128))
        p.dma(wsb.rearrange("p (co i) -> p co i", i=D),
              st.W.rearrange("(co p) i -> p co i", p=128))
        p.dma(brw[:], st.b.rearrange("(a n) -> a n", a=1))

        p.tr(pmt[0:100, 0:4], ldn[:], c.ident_f[0:4, 0:4])
        p.act(ldT[:], pmt[0:100, 0:4], AF.Copy)
        p.tr(pmt[0:4, 0:50], lun[:], c.ident_f[0:50, 0:50])
        p.act(luT[:], pmt[0:4, 0:50], AF.Copy)
        for half in range(2):
            p.mm(pm[0:4, :], ldT[:], dax[:, sl(half, 512)])
            p.act(m1[:, sl(half, 512)], pm[0:4, :], AF.Copy)
        for half in range(2):
            p.mm(pm[0:50, :], luT[:], m1[:, sl(half, 512)])
            p.act(m2[:, sl(half, 512)], pm[0:50, :], AF.Copy)
        for ci in range(NC):
            p.tr(pmt[0:50, :], upn[:, sl(ci, 50)], c.ident_f[:])
            p.act(upT[:, sl(ci, 128)], pmt[0:50, 0:128], AF.Copy)
        for ci in range(NC):
            for half in range(2):
                p.mm(pm[:], m2[:, sl(ci, 128)], upT[:, sl(half, 512)])
                for cq in range(4):
                    co = half * 4 + cq
                    p.tr(pmt[:], wsb[:, co * D + ci * 128:co * D + (ci + 1) * 128],
                         c.ident_f[:])
                    p.act(wtt[:], pmt[:], AF.Copy, scale=0.5)
                    p.stt(wet[:, ci * D + co * 128:ci * D + (co + 1) * 128],
                          pm[:, sl(cq, 128)], 0.5, wtt[:], ALU.mult, ALU.add)
        for half in range(2):
            p.mm(pm[:], c.ones_f[0:1, :], brw[:, sl(half, 512)])
            p.act(hbs[:, sl(half, 512)], pm[:], AF.Copy, scale=0.5)
        p.dma(st.wet_scr, wet[:])
        p.dma(st.hb_scr, hbs[:])
        p.flush()


def emit_l1ffn(p, c, st):
    nc = p.nc
    with ExitStack() as ex:
        zhT = p.sb(ex, "lf_zhT", [128, D], BF16)
        wet = p.sb(ex, "lf_wet", [128, NC * D], BF16)
        hbs = p.sb(ex, "lf_hbs", [128, D], F32)
        tmp = p.sb(ex, "lf_tmp", [128, 512], F32)
        pm = p.ps(ex, "lf_pm", [128, 512], F32)
        p.dma(wet[:], st.wet_scr)
        p.dma(hbs[:], st.hb_scr)
        for t in range(NT):
            ln_tile(p, c, t,
                    bass.AP(tensor=zhT, offset=0, ap=[[D, 128], [128, NC], [1, TT]]))
            for half in range(2):
                for ci in range(NC):
                    p.mm(pm[:], zhT[:, sl(ci, 128)],
                         wet[:, ci * D + half * 512:ci * D + (half + 1) * 512],
                         start=(ci == 0), stop=(ci == NC - 1))
                p.tt(tmp[:], pm[:], hbs[:, sl(half, 512)], op=ALU.add)
                xsl = c.x[:, t * D + half * 512:t * D + (half + 1) * 512]
                p.tt(xsl, xsl, tmp[:], op=ALU.add)
        p.flush()


def emit_attn(p, c, st):
    nc = p.nc
    with ExitStack() as ex:
        cqT = p.sb(ex, "at_cqT", [128, 2 * S], BF16)
        ckT = p.sb(ex, "at_ckT", [128, 2 * S], BF16)
        qrr = p.sb(ex, "at_qrr", [128, NT * 512], BF16)
        vsb = p.sb(ex, "at_vsb", [128, NT * 1040], BF16)
        krT = p.sb(ex, "at_krT", [32, S], BF16)
        wuq = p.sb(ex, "at_wuq", [128, 2 * D], BF16)
        wuk = p.sb(ex, "at_wuk", [128, 2 * D], BF16)
        wqr = p.sb(ex, "at_wqr", [128, 2 * 512], BF16)
        wo = p.sb(ex, "at_wo", [128, NC * D], BF16)

        p.dma(wuq.rearrange("p (cc f) -> p cc f", f=D),
              st.W_UQ.rearrange("(cc p) f -> p cc f", p=128), eng="gpsimd")
        p.dma(wuk.rearrange("p (cc f) -> p cc f", f=D),
              st.W_UK.rearrange("(cc p) f -> p cc f", p=128), eng="gpsimd")
        p.dma(wqr.rearrange("p (cc f) -> p cc f", f=512),
              st.W_QR.rearrange("(cc p) f -> p cc f", p=128), eng="gpsimd")
        p.dma(wo.rearrange("p (cc f) -> p cc f", f=D),
              st.W_O.rearrange("(cc p) f -> p cc f", p=128), eng="gpsimd")

        with ExitStack() as ex1:
            zhT = p.sb(ex1, "at_zhT", [128, D], BF16)
            wdq = p.sb(ex1, "at_wdq", [128, NC * DL], BF16)
            wdk = p.sb(ex1, "at_wdk", [128, NC * DL], BF16)
            wuv = p.sb(ex1, "at_wuv", [128, 2 * D], BF16)
            wkr = p.sb(ex1, "at_wkr", [128, NC * 32], BF16)
            cost = p.sb(ex1, "at_cos", [128, 16], F32)
            sint = p.sb(ex1, "at_sin", [128, 16], F32)
            rt1 = p.sb(ex1, "at_rt1", [128, 256], F32)
            rt2 = p.sb(ex1, "at_rt2", [128, 256], F32)
            krr = p.sb(ex1, "at_krr", [128, 32], BF16)
            pm = p.ps(ex1, "at_pm", [128, 512], F32)
            pmb = p.ps(ex1, "at_pmb", [128, 512], BF16)

            p.dma(wdq.rearrange("p (cc f) -> p cc f", f=DL),
                  st.W_DQ.rearrange("(cc p) f -> p cc f", p=128), eng="gpsimd")
            p.dma(wdk.rearrange("p (cc f) -> p cc f", f=DL),
                  st.W_DKV.rearrange("(cc p) f -> p cc f", p=128), eng="gpsimd")
            p.dma(wuv.rearrange("p (cc f) -> p cc f", f=D),
                  st.W_UV.rearrange("(cc p) f -> p cc f", p=128), eng="gpsimd")
            p.dma(wkr.rearrange("p (cc f) -> p cc f", f=32),
                  st.W_KR.rearrange("(cc p) f -> p cc f", p=128), eng="gpsimd")

            for t in range(NT):
                ln_tile(p, c, t,
                        bass.AP(tensor=zhT, offset=0, ap=[[D, 128], [128, NC], [1, TT]]))
                p.dma(cost[:], st.cos_t[sl(t, 128), :])
                p.dma(sint[:], st.sin_t[sl(t, 128), :])
                for lc in range(2):
                    for fc in range(NC):
                        p.mm(pm[:, 0:128],
                             wdq[:, fc * DL + lc * 128:fc * DL + (lc + 1) * 128],
                             zhT[:, sl(fc, 128)],
                             start=(fc == 0), stop=(fc == NC - 1))
                    p.act(cqT[:, lc * S + t * TT:lc * S + (t + 1) * TT],
                          pm[:, 0:128], AF.Copy)
                    for fc in range(NC):
                        p.mm(pm[:, 0:128],
                             wdk[:, fc * DL + lc * 128:fc * DL + (lc + 1) * 128],
                             zhT[:, sl(fc, 128)],
                             start=(fc == 0), stop=(fc == NC - 1))
                    p.act(ckT[:, lc * S + t * TT:lc * S + (t + 1) * TT],
                          pm[:, 0:128], AF.Copy)
                for fc in range(NC):
                    p.mm(pm[:, 0:32], zhT[:, sl(fc, 128)], wkr[:, sl(fc, 32)],
                         start=(fc == 0), stop=(fc == NC - 1))
                pe = pm[:, 0:32].rearrange("p (f two) -> p f two", two=2)
                p.tt(rt1[:, 0:16], pe[:, :, 0], cost[:])
                p.tt(rt2[:, 0:16], pe[:, :, 1], sint[:])
                p.tt(krr.rearrange("p (f two) -> p f two", two=2)[:, :, 0],
                     rt1[:, 0:16], rt2[:, 0:16], op=ALU.subtract)
                p.tt(rt1[:, 0:16], pe[:, :, 0], sint[:])
                p.tt(rt2[:, 0:16], pe[:, :, 1], cost[:])
                p.tt(krr.rearrange("p (f two) -> p f two", two=2)[:, :, 1],
                     rt1[:, 0:16], rt2[:, 0:16], op=ALU.add)
                p.tr(pmb[0:32, 0:128], krr[:], c.ident_b[:])
                p.act(krT[:, sl(t, 128)], pmb[0:32, 0:128], AF.Copy)
                for lc in range(2):
                    p.mm(pm[:], cqT[:, lc * S + t * TT:lc * S + (t + 1) * TT],
                         wqr[:, sl(lc, 512)],
                         start=(lc == 0), stop=(lc == 1))
                pe = pm.rearrange("p (h f two) -> p h f two", two=2, f=16)
                cb = cost[:, None, :].broadcast_to([128, 16, 16])
                sb = sint[:, None, :].broadcast_to([128, 16, 16])
                r3 = rt1.rearrange("p (h f) -> p h f", f=16)
                r4 = rt2.rearrange("p (h f) -> p h f", f=16)
                qv = qrr[:, sl(t, 512)].rearrange("p (h f two) -> p h f two", two=2, f=16)
                p.tt(r3, pe[:, :, :, 0], cb)
                p.tt(r4, pe[:, :, :, 1], sb)
                p.tt(qv[:, :, :, 0], r3, r4, op=ALU.subtract)
                p.tt(r3, pe[:, :, :, 0], sb)
                p.tt(r4, pe[:, :, :, 1], cb)
                p.tt(qv[:, :, :, 1], r3, r4, op=ALU.add)
                for half in range(2):
                    for lc in range(2):
                        p.mm(pm[:], ckT[:, lc * S + t * TT:lc * S + (t + 1) * TT],
                             wuv[:, lc * D + half * 512:lc * D + (half + 1) * 512],
                             start=(lc == 0), stop=(lc == 1))
                    dst = bass.AP(tensor=vsb, offset=t * 1040 + half * 520,
                                  ap=[[NT * 1040, 128], [65, 8], [1, 64]])
                    p.act(dst, pm.rearrange("p (h f) -> p h f", f=64), AF.Copy)
                p.memset(bass.AP(tensor=vsb, offset=t * 1040 + 64,
                                 ap=[[NT * 1040, 128], [65, 16]]), 1.0)
            p.flush()

        with ExitStack() as ex2:
            atT = p.sb(ex2, "at_atT", [128, S], BF16)
            qh = p.sb(ex2, "at_qh", [64, S], BF16)
            kh = p.sb(ex2, "at_kh", [64, S], BF16)
            qrh = p.sb(ex2, "at_qrh", [32, S], BF16)
            maskw = p.sb(ex2, "at_maskw", [128, 4 * 512], F32)
            scb = [p.sb(ex2, "at_scb", [128, 512], F32) for _ in range(3)]
            etb = [p.sb(ex2, "at_etb", [128, 512], BF16) for _ in range(3)]
            rrs = p.sb(ex2, "at_rrs", [1, 512], F32)
            rsum = p.sb(ex2, "at_rsum", [1, 512], F32)
            rbs = p.sb(ex2, "at_rbs", [64, 512], F32)
            psc = [p.ps(ex2, "at_psc", [128, 512], F32) for _ in range(3)]
            pmav = p.ps(ex2, "at_pmav", [65, 512], F32)
            pm2 = p.ps(ex2, "at_pm2", [128, 1024], F32)
            pmr = p.ps(ex2, "at_pmr", [64, 512], F32)

            p.dma(maskw[:], st.mask512)

            ATT_LVL = int(os.environ.get("ATT_LVL", "9"))
            for h_i in range(NH):
                hc, hp = h_i // 2, h_i % 2
                for qc in range(4):
                    for lc in range(2):
                        p.mm(pm2[0:64, 0:512],
                             wuq[:, lc * D + h_i * 64:lc * D + (h_i + 1) * 64],
                             cqT[:, lc * S + qc * 512:lc * S + (qc + 1) * 512],
                             start=(lc == 0), stop=(lc == 1))
                    p.act(qh[:, sl(qc, 512)], pm2[0:64, 0:512], AF.Copy)
                    for lc in range(2):
                        p.mm(pm2[0:64, 512:1024],
                             wuk[:, lc * D + h_i * 64:lc * D + (h_i + 1) * 64],
                             ckT[:, lc * S + qc * 512:lc * S + (qc + 1) * 512],
                             start=(lc == 0), stop=(lc == 1))
                    p.act(kh[:, sl(qc, 512)], pm2[0:64, 512:1024], AF.Copy)
                for qc in range(4):
                    for tk in range(4):
                        t = qc * 4 + tk
                        p.tr(c.pm_trb[0:32, sl(tk, 128)],
                             qrr[:, t * 512 + h_i * 32:t * 512 + (h_i + 1) * 32],
                             c.ident_b[:])
                    p.act(qrh[:, sl(qc, 512)], c.pm_trb[0:32, 0:512], AF.Copy)

                if ATT_LVL < 1:
                    continue
                for qs in range(4):
                    nkt = 4 * qs + 4
                    qsl = slice(qs * 512, (qs + 1) * 512)

                    def e_score(kt):
                        b = kt % 3
                        p.mm(psc[b][:], kh[:, sl(kt, 128)], qh[:, qsl],
                             start=True, stop=False)
                        p.mm(psc[b][:], krT[:, sl(kt, 128)], qrh[:, qsl],
                             start=False, stop=True)

                    def e_sm(kt):
                        b = kt % 3
                        p.ts(scb[b][:], psc[b][:], 80.0, ALU.min,
                             s2=-80.0, op1=ALU.max)
                        if kt >= 4 * qs:
                            r = kt - 4 * qs
                            p.tt(scb[b][:], scb[b][:],
                                 maskw[:, sl(r, 512)], op=ALU.add)
                        p.act(etb[b][:], scb[b][:], AF.Exp, scale=INV_SQRT_DQK)

                    def e_av(kt):
                        b = kt % 3
                        p.mm(pmav[:],
                             vsb[:, kt * 1040 + h_i * 65:kt * 1040 + (h_i + 1) * 65],
                             etb[b][:], start=(kt == 0), stop=(kt == nkt - 1))

                    for kt in range(nkt):
                        e_score(kt)
                        if ATT_LVL >= 2:
                            e_sm(kt)
                        if ATT_LVL >= 3 and kt >= 2:
                            e_av(kt - 2)
                    if ATT_LVL < 3:
                        continue
                    e_av(nkt - 2)
                    e_av(nkt - 1)
                    if ATT_LVL < 4:
                        continue
                    p.act(rsum[:], pmav[64:65, :], AF.Copy)
                    p.recip(rrs[:], rsum[:])
                    p.mm(pmr[:], c.ones_f[0:1, 0:64], rrs[:])
                    p.act(rbs[:], pmr[:], AF.Copy)
                    p.tt(atT[hp * 64:(hp + 1) * 64, qsl],
                         pmav[0:64, :], rbs[:], op=ALU.mult)
                if hp == 1 and ATT_LVL >= 5:
                    for t in range(NT):
                        for half in range(2):
                            hx = half * 512
                            p.mm(pm2[:, hx:hx + 512], atT[:, sl(t, TT)],
                                 wo[:, hc * D + hx:hc * D + hx + 512])
                        for half in range(2):
                            hx = half * 512
                            xsl = c.x[:, t * D + hx:t * D + hx + 512]
                            p.tt(xsl, xsl, pm2[:, hx:hx + 512], op=ALU.add)
            p.flush()


class Ctx:
    pass


def build_nc(parts=ALL_PARTS):
    nc = bass.Bass("TRN2", target_bir_lowering=False)
    st = Ctx()
    st.x_in = nc.dram_tensor("x_in", [S, D], F32, kind="ExternalInput").ap()
    st.out = nc.dram_tensor("out", [S, D], F32, kind="ExternalOutput").ap()
    st.ident_in = nc.dram_tensor("ident_in", [128, 128], F32, kind="ExternalInput").ap()
    st.mask_in = nc.dram_tensor("mask_in", [128, 128], F32, kind="ExternalInput").ap()
    st.mask512 = nc.dram_tensor("mask512", [128, 4 * 512], F32, kind="ExternalInput").ap()
    st.cos_t = nc.dram_tensor("cos_t", [S, 16], F32, kind="ExternalInput").ap()
    st.sin_t = nc.dram_tensor("sin_t", [S, 16], F32, kind="ExternalInput").ap()
    st.A = nc.dram_tensor("L0_attn_A", [D, 16], F32, kind="ExternalInput").ap()
    st.W_B = nc.dram_tensor("L0_attn_W_B", [16, D], F32, kind="ExternalInput").ap()
    st.b_B = nc.dram_tensor("L0_attn_b_B", [16], F32, kind="ExternalInput").ap()
    st.W_C = nc.dram_tensor("L0_attn_W_C", [16, D], F32, kind="ExternalInput").ap()
    st.b_C = nc.dram_tensor("L0_attn_b_C", [16], F32, kind="ExternalInput").ap()
    st.W_Delta = nc.dram_tensor("L0_attn_W_Delta", [1, D], F32, kind="ExternalInput").ap()
    st.b_Delta = nc.dram_tensor("L0_attn_b_Delta", [1], F32, kind="ExternalInput").ap()
    st.bias_Delta = nc.dram_tensor("L0_attn_bias_Delta", [1], F32, kind="ExternalInput").ap()
    st.gate_W = nc.dram_tensor("L0_ffn_gate_W", [D, 8], F32, kind="ExternalInput").ap()
    st.ffn_W1 = nc.dram_tensor("L0_ffn_W1", [NE, D, DHID], F32, kind="ExternalInput").ap()
    st.ffn_b1 = nc.dram_tensor("L0_ffn_b1", [NE, DHID], F32, kind="ExternalInput").ap()
    st.ffn_W2 = nc.dram_tensor("L0_ffn_W2", [NE, DHID, D], F32, kind="ExternalInput").ap()
    st.ffn_b2 = nc.dram_tensor("L0_ffn_b2", [NE, D], F32, kind="ExternalInput").ap()
    st.W_DQ = nc.dram_tensor("L1_attn_W_DQ", [D, DL], F32, kind="ExternalInput").ap()
    st.W_UQ = nc.dram_tensor("L1_attn_W_UQ", [DL, D], F32, kind="ExternalInput").ap()
    st.W_QR = nc.dram_tensor("L1_attn_W_QR", [DL, 512], F32, kind="ExternalInput").ap()
    st.W_DKV = nc.dram_tensor("L1_attn_W_DKV", [D, DL], F32, kind="ExternalInput").ap()
    st.W_UK = nc.dram_tensor("L1_attn_W_UK", [DL, D], F32, kind="ExternalInput").ap()
    st.W_UV = nc.dram_tensor("L1_attn_W_UV", [DL, D], F32, kind="ExternalInput").ap()
    st.W_KR = nc.dram_tensor("L1_attn_W_KR", [D, 32], F32, kind="ExternalInput").ap()
    st.W_O = nc.dram_tensor("L1_attn_W_O", [D, D], F32, kind="ExternalInput").ap()
    st.W = nc.dram_tensor("L1_ffn_W", [D, D], F32, kind="ExternalInput").ap()
    st.b = nc.dram_tensor("L1_ffn_b", [D], F32, kind="ExternalInput").ap()
    st.lora_down = nc.dram_tensor("L1_ffn_lora_down", [4, 100], F32, kind="ExternalInput").ap()
    st.lora_up = nc.dram_tensor("L1_ffn_lora_up", [50, 4], F32, kind="ExternalInput").ap()
    st.down_aux = nc.dram_tensor("L1_ffn_down_aux", [100, D], F32, kind="ExternalInput").ap()
    st.up_aux = nc.dram_tensor("L1_ffn_up_aux", [D, 50], F32, kind="ExternalInput").ap()
    st.bm_scr = [nc.dram_tensor(f"bm_scr{i}", [128, 16], F32, kind="Internal").ap()
                 for i in range(2)]
    st.cm_scr = [nc.dram_tensor(f"cm_scr{i}", [128, 16], F32, kind="Internal").ap()
                 for i in range(2)]
    st.wet_scr = nc.dram_tensor("wet_scr", [128, NC * D], BF16, kind="Internal").ap()
    st.hb_scr = nc.dram_tensor("hb_scr", [128, D], F32, kind="Internal").ap()

    c = Ctx()
    with ExitStack() as top:
        semnames = ("tensor", "vector", "scalar", "gpsimd",
                    "dsync", "dgp", "dscal", "dvec")
        sems = {k: top.enter_context(nc.semaphore("sem_" + k)) for k in semnames}
        p = Prog(nc, sems)
        c.x = p.sb(top, "x_sb", [128, NT * D], F32)
        c.ident_f = p.sb(top, "ident_f", [128, 128], F32)
        c.ident_b = p.sb(top, "ident_b", [128, 128], BF16)
        c.maskc = p.sb(top, "maskc", [128, 128], F32)
        c.ones_f = p.sb(top, "ones_f", [1, 128], F32)
        c.zhat = p.sb(top, "zhat", [128, D], BF16)
        c.scr = p.sb(top, "ln_scr", [128, D], F32)
        c.sumc = p.sb(top, "ln_sum", [128, 1], F32)
        c.sqc = p.sb(top, "ln_sq", [128, 1], F32)
        c.mc = p.sb(top, "ln_m", [128, 1], F32)
        c.msq = p.sb(top, "ln_msq", [128, 1], F32)
        c.varc = p.sb(top, "ln_var", [128, 1], F32)
        c.lnc = p.sb(top, "ln_ln", [128, 1], F32)
        c.rstdc = p.sb(top, "ln_rstd", [128, 1], F32)
        c.epscol = p.sb(top, "epscol", [128, 1], F32)
        c.onecol = p.sb(top, "onecol", [128, 1], F32)
        c.pm_trb = p.ps(top, "pm_trb", [128, D], BF16)
        if True:
            p.dma(c.x.rearrange("p (n d) -> p n d", d=D),
                  st.x_in.rearrange("(n p) d -> p n d", p=128))
            p.dma(c.ident_f[:], st.ident_in)
            p.dma(c.ident_b[:], st.ident_in, eng="gpsimd")
            p.dma(c.maskc[:], st.mask_in)
            p.memset(c.ones_f[:], 1.0)
            p.memset(c.epscol[:], 1.0e-5)
            p.memset(c.onecol[:], 1.0)
            p.flush()
            for part in parts:
                if part in ("moe1", "moe2"):
                    emit_moe(p, c, st)
                elif part == "mamba":
                    emit_mamba(p, c, st)
                elif part == "ffn1":
                    emit_l1ffn_prep(p, c, st)
                    emit_l1ffn(p, c, st)
                elif part == "ffn1prep":
                    emit_l1ffn_prep(p, c, st)
                elif part == "ffn2":
                    emit_l1ffn(p, c, st)
                elif part == "attn":
                    emit_attn(p, c, st)
            p.dma(st.out.rearrange("(n p) d -> p n d", p=128),
                  c.x.rearrange("p (n d) -> p n d", d=D))
            p.flush(final=True)
    return nc


def host_consts():
    ident = np.eye(128, dtype=np.float32)
    mask = np.where(np.arange(128)[:, None] <= np.arange(128)[None, :],
                    0.0, NEG).astype(np.float32)
    inv = (10000.0 ** (-np.arange(0, 32, 2, dtype=np.float64) / 32.0))
    ang = np.arange(S, dtype=np.float64)[:, None] * inv[None, :]
    cos_t = np.cos(ang).astype(np.float32)
    sin_t = np.sin(ang).astype(np.float32)
    kk = np.arange(128)[:, None]
    qq = np.arange(512)[None, :]
    m512 = np.concatenate(
        [np.where(qq >= kk + 128 * r, 0.0, NEG) for r in range(4)],
        axis=1).astype(np.float32)
    return {"ident_in": ident, "mask_in": mask, "cos_t": cos_t, "sin_t": sin_t,
            "mask512": m512}


def make_in_maps(inputs, xfull):
    w = {k: np.ascontiguousarray(np.asarray(v, dtype=np.float32))
         for k, v in inputs.items() if k != "L0_attn_sequence"}
    consts = host_consts()
    maps = []
    for b in range(8):
        m = {"x_in": np.ascontiguousarray(xfull[b])}
        m.update(w)
        m.update(consts)
        maps.append(m)
    return maps


def kernel(**inputs):
    xfull = np.ascontiguousarray(np.asarray(inputs["L0_attn_sequence"], np.float32))
    nc = build_nc(ALL_PARTS)
    maps = make_in_maps(inputs, xfull)
    res = run_bass_kernel_spmd(nc, maps, core_ids=list(range(8)))
    return np.stack([res.results[b]["out"] for b in range(8)], axis=0)


# version 07
# speedup vs baseline: 1.0014x; 1.0014x over previous
import math
import os
from contextlib import ExitStack

KSER = int(os.environ.get("KSER", "0"))

import numpy as np
import ml_dtypes

import concourse.bass as bass
import concourse.mybir as mybir
from concourse.bass_utils import run_bass_kernel_spmd

F32 = mybir.dt.float32
BF16 = mybir.dt.bfloat16
AF = mybir.ActivationFunctionType
ALU = mybir.AluOpType
AX = mybir.AxisListType

S, D, TT = 2048, 1024, 128
NT = S // TT
NC = D // 128
NH, DH, DHR, DL = 16, 64, 32, 256
NE, DHID = 8, 2048
INV_SQRT_DQK = 1.0 / math.sqrt(96.0)
NEG = -1.0e30

ALL_PARTS = ("moe1", "mamba", "moe2", "ffn1", "attn", "ffn2")

_UID = [0]


def _nm(s):
    _UID[0] += 1
    return f"{s}_{_UID[0]}"


class Op:
    __slots__ = ("fn", "deps", "dma", "dma_idx", "need_inc", "mval")

    def __init__(self, fn, dma, dma_idx):
        self.fn = fn
        self.deps = set()
        self.dma = dma
        self.dma_idx = dma_idx
        self.need_inc = False
        self.mval = 0


class Prog:
    CENG = ("tensor", "vector", "scalar", "gpsimd")
    DSEM = {"sync": "dsync", "gpsimd": "dgp", "scalar": "dscal", "vector": "dvec"}
    QUEUES = ("sync", "tensor", "vector", "scalar", "gpsimd")

    def __init__(self, nc, sems):
        self.nc = nc
        self.sems = sems
        self.ops = {k: [] for k in self.QUEUES}
        self.ndma = {k: 0 for k in self.QUEUES}
        self.res = {}
        self.addr = {}
        self.rowel = {}

    def _keys(self, ap):
        t = ap.tensor
        nm = t.name
        if nm not in self.addr:
            return [("D", nm)]
        esz = mybir.dt.size(ap.dtype)
        row = self.rowel[nm]
        off = ap.offset % row
        ext = 0
        for s_, n_ in list(ap.ap)[1:]:
            if n_ > 1 and s_ > 0:
                ext += (n_ - 1) * s_
        lo = self.addr[nm] + off * esz
        hi = lo + ext * esz + esz
        if getattr(t, "space", None) is not None and "PSUM" in str(t.space).upper():
            return [("P", b) for b in range(lo // 2048, (hi - 1) // 2048 + 1)]
        return [("S", g) for g in range(lo // 512, (hi - 1) // 512 + 1)]

    def emit(self, eng, fn, reads=(), writes=(), dma=False):
        ops = self.ops[eng]
        idx = len(ops)
        if dma:
            op = Op(fn, True, self.ndma[eng])
            self.ndma[eng] += 1
        else:
            op = Op(fn, False, 0)
        tok = (eng, idx)
        if KSER:
            for q in self.QUEUES:
                if self.ops[q] and not (q == eng):
                    op.deps.add((q, len(self.ops[q]) - 1))
            if idx:
                op.deps.add((eng, idx - 1))

        def add(t_):
            if t_ is not None and t_ != tok:
                op.deps.add(t_)

        rkeys = [k for ap in reads if ap is not None for k in self._keys(ap)]
        wkeys = [k for ap in writes if ap is not None for k in self._keys(ap)]
        for k in rkeys:
            e = self.res.get(k)
            if e:
                add(e["w"])
        for k in wkeys:
            e = self.res.get(k)
            if e:
                add(e["w"])
                for tkn in e["r"]:
                    add(tkn)
        ops.append(op)
        for k in rkeys:
            self.res.setdefault(k, {"w": None, "r": []})["r"].append(tok)
        for k in wkeys:
            self.res[k] = {"w": tok, "r": [tok]} if k in rkeys else {"w": tok, "r": []}
        return tok

    def _wait_needed(self, cq, pq, pop):
        if pop.dma:
            return True
        if KSER:
            return True
        if pq == cq and pq == "tensor":
            return False
        return True

    def flush(self, final=False):
        if not final:
            return
        ops = self.ops
        for cq in self.QUEUES:
            for cop in ops[cq]:
                for (pq, pi) in cop.deps:
                    pop = ops[pq][pi]
                    if not pop.dma and self._wait_needed(cq, pq, pop):
                        pop.need_inc = True
        for q in self.QUEUES:
            for op_ in reversed(ops[q]):
                if not op_.dma:
                    op_.need_inc = True
                    break
        for q in self.QUEUES:
            m = 0
            for op_ in ops[q]:
                if (not op_.dma) and op_.need_inc:
                    m += 1
                op_.mval = m
        sched = {q: [] for q in self.QUEUES}
        last_wait = {}
        for q in self.QUEUES:
            for op_ in ops[q]:
                deps = {}
                for (pq, pi) in op_.deps:
                    pop = ops[pq][pi]
                    if not self._wait_needed(q, pq, pop):
                        continue
                    if pop.dma:
                        sn, v = self.DSEM[pq], 16 * (pop.dma_idx + 1)
                    else:
                        sn, v = pq, pop.mval
                        assert pop.need_inc and v > 0
                    if deps.get(sn, 0) < v:
                        deps[sn] = v
                waits = []
                for sn, v in sorted(deps.items()):
                    lw = last_wait.get((q, sn), 0)
                    if v <= lw:
                        continue
                    waits.append((self.sems[sn], v))
                    last_wait[(q, sn)] = v
                if op_.dma:
                    inc = (self.sems[self.DSEM[q]], 16)
                elif op_.need_inc:
                    inc = (self.sems[q], 1)
                else:
                    inc = None
                sched[q].append((waits, op_.fn, inc))
        fwaits = []
        for q in self.QUEUES:
            if ops[q]:
                mv = ops[q][-1].mval
                if mv:
                    fwaits.append((self.sems[q], mv))
            if self.ndma[q]:
                fwaits.append((self.sems[self.DSEM[q]], 16 * self.ndma[q]))
        sched["sync"].append((fwaits, None, None))

        def runner(lst):
            def run(e):
                for waits, fn, inc in lst:
                    for sh, v in waits:
                        e.wait_ge(sh, v)
                    if fn is not None:
                        if inc is not None:
                            fn(e).then_inc(inc[0], inc[1])
                        else:
                            fn(e)
            return run

        with self.nc.Block() as block:
            for eng in self.QUEUES:
                if sched[eng]:
                    getattr(block, eng)(runner(sched[eng]))

    @staticmethod
    def _isap(x):
        return hasattr(x, "tensor")

    def dma(self, out, in_, eng="sync"):
        self.emit(eng, lambda e: e.dma_start(out, in_),
                  reads=[in_], writes=[out], dma=True)

    def mm(self, out, lhsT, rhs, start=True, stop=True):
        rd = [lhsT, rhs] + ([out] if not start else [])
        self.emit("tensor",
                  lambda e: e.matmul(out, lhsT, rhs, start=start, stop=stop),
                  reads=rd, writes=[out])

    def tr(self, out, in_, ident):
        self.emit("tensor", lambda e: e.transpose(out, in_, ident),
                  reads=[in_, ident], writes=[out])

    def tt(self, out, a, b, op=ALU.mult, eng="vector"):
        self.emit(eng, lambda e: e.tensor_tensor(out=out, in0=a, in1=b, op=op),
                  reads=[a, b], writes=[out])

    def ts(self, out, in0, s1, op0, s2=None, op1=ALU.mult, accum_out=None, eng="vector"):
        rd = [in0] + [x for x in (s1, s2) if self._isap(x)]
        wr = [out] + ([accum_out] if accum_out is not None else [])
        if s2 is None and accum_out is None:
            self.emit(eng, lambda e: e.tensor_scalar(
                out=out, in0=in0, scalar1=s1, scalar2=None, op0=op0),
                reads=rd, writes=wr)
        else:
            self.emit(eng, lambda e: e.tensor_scalar(
                out=out, in0=in0, scalar1=s1, scalar2=s2, op0=op0, op1=op1,
                accum_out=accum_out), reads=rd, writes=wr)

    def stt(self, out, in0, scalar, in1, op0=ALU.mult, op1=ALU.add, eng="vector"):
        rd = [in0, in1] + ([scalar] if self._isap(scalar) else [])
        self.emit(eng, lambda e: e.scalar_tensor_tensor(
            out=out, in0=in0, scalar=scalar, in1=in1, op0=op0, op1=op1),
            reads=rd, writes=[out])

    def act(self, out, in_, func, bias=0.0, scale=1.0, accum_out=None):
        rd = [in_] + [x for x in (bias, scale) if self._isap(x)]
        wr = [out] + ([accum_out] if accum_out is not None else [])
        self.emit("scalar", lambda e: e.activation(
            out=out, in_=in_, func=func, bias=bias, scale=scale,
            accum_out=accum_out), reads=rd, writes=wr)

    def red(self, out, in_, op=ALU.add):
        self.emit("vector",
                  lambda e: e.tensor_reduce(out=out, in_=in_, axis=AX.X, op=op),
                  reads=[in_], writes=[out])

    def scan(self, out, d0, d1, initial):
        self.emit("vector", lambda e: e.tensor_tensor_scan(
            out=out, data0=d0, data1=d1, initial=initial,
            op0=ALU.mult, op1=ALU.add), reads=[d0, d1, initial], writes=[out])

    def copy(self, out, in_, eng="vector"):
        self.emit(eng, lambda e: e.tensor_copy(out, in_),
                  reads=[in_], writes=[out])

    def memset(self, ap, v, eng="vector"):
        self.emit(eng, lambda e: e.memset(ap, v), writes=[ap])

    def recip(self, out, in_, eng="vector"):
        self.emit(eng, lambda e: e.reciprocal(out, in_),
                  reads=[in_], writes=[out])

    def sb(self, ex, name, shape, dtype):
        base = self.nc.sbuf_base
        h = ex.enter_context(self.nc.sbuf_tensor(_nm(name), shape, dtype))
        self.addr[h.name] = base
        self.rowel[h.name] = int(np.prod(shape[1:])) if len(shape) > 1 else 1
        return h

    def ps(self, ex, name, shape, dtype):
        base = self.nc.psum_base * 2048
        h = ex.enter_context(self.nc.psum_tensor(_nm(name), shape, dtype))
        self.addr[h.name] = base
        self.rowel[h.name] = int(np.prod(shape[1:])) if len(shape) > 1 else 1
        return h


def sl(t, w):
    return slice(t * w, (t + 1) * w)


def ln_tile(p, c, t, zhatT_dest, half_res=False, f32_scr=False):
    xs = c.x[:, sl(t, D)]
    p.red(c.sumc[:], xs)
    p.act(c.scr[:], xs, AF.Square, accum_out=c.sqc[:])
    p.ts(c.mc[:], c.sumc[:], 1.0 / D, ALU.mult)
    p.ts(c.msq[:], c.mc[:], c.mc[:], ALU.mult)
    p.stt(c.varc[:], c.sqc[:], 1.0 / D, c.msq[:], ALU.mult, ALU.subtract)
    p.act(c.lnc[:], c.varc[:], AF.Ln, bias=c.epscol[:])
    p.act(c.rstdc[:], c.lnc[:], AF.Exp, scale=-0.5)
    p.ts(c.zhat[:], xs, c.mc[:], ALU.subtract, s2=c.rstdc[:], op1=ALU.mult)
    if f32_scr:
        p.ts(c.scr[:], xs, c.mc[:], ALU.subtract, s2=c.rstdc[:], op1=ALU.mult)
    if half_res:
        p.stt(xs, c.zhat[:], 0.5, xs, ALU.mult, ALU.add)
    for ci in range(NC):
        p.tr(c.pm_trb[:, sl(ci, 128)], c.zhat[:, sl(ci, 128)], c.ident_b[:])
    p.act(zhatT_dest, c.pm_trb[:], AF.Copy)


def emit_moe(p, c, st):
    nc = p.nc
    with ExitStack() as ex:
        zT = p.sb(ex, "moe_zT", [128, NC * S], BF16)
        gate = p.sb(ex, "moe_gate", [128, NC * 8], F32)
        cw = p.sb(ex, "moe_cw", [128, NT * 8], F32)
        g8 = p.sb(ex, "moe_g8", [128, 8], F32)
        gm = p.sb(ex, "moe_gm", [128, 6], F32)
        w1 = p.sb(ex, "moe_w1", [128, NC * DHID], BF16)
        w2 = p.sb(ex, "moe_w2", [128, 16 * D], BF16)
        h = p.sb(ex, "moe_h", [128, 16 * 512], BF16)
        b1n = p.sb(ex, "moe_b1n", [16, 128], F32)
        b1c = p.sb(ex, "moe_b1c", [128, 16], F32)
        b2s = p.sb(ex, "moe_b2s", [8, D], F32)
        cwT = p.sb(ex, "moe_cwT", [8, 128], F32)
        zTf = p.sb(ex, "moe_zTf", [128, D], F32)
        pm = p.ps(ex, "moe_pm", [128, 512], F32)
        pmo = p.ps(ex, "moe_pmo", [128, 512], F32)
        pms = p.ps(ex, "moe_pms", [128, 128], F32)
        pmtf = p.ps(ex, "moe_pmtf", [128, D], F32)

        p.dma(gate.rearrange("p (cc e) -> p cc e", e=8),
              st.gate_W.rearrange("(cc p) e -> p cc e", p=128))
        p.dma(b2s[:], st.ffn_b2)

        for t in range(NT):
            ln_tile(p, c, t,
                    bass.AP(tensor=zT, offset=t * TT,
                            ap=[[NC * S, 128], [S, NC], [1, TT]]),
                    half_res=True, f32_scr=True)
            for ci in range(NC):
                p.tr(pmtf[:, sl(ci, 128)], c.scr[:, sl(ci, 128)], c.ident_f[:])
            p.act(zTf[:], pmtf[:], AF.Copy)
            for fc in range(NC):
                p.mm(pms[:, 0:8], zTf[:, sl(fc, 128)],
                     gate[:, sl(fc, 8)], start=(fc == 0), stop=(fc == NC - 1))
            p.red(gm[:, 0:1], pms[:, 0:8], op=ALU.max)
            p.ts(g8[:], pms[:, 0:8], gm[:, 0:1], ALU.subtract)
            p.act(g8[:], g8[:], AF.Exp)
            p.red(gm[:, 1:2], g8[:])
            p.recip(gm[:, 2:3], gm[:, 1:2])
            p.ts(g8[:], g8[:], gm[:, 2:3], ALU.mult)
            p.red(gm[:, 0:1], g8[:], op=ALU.max)
            p.ts(c.scr[:, 0:8], g8[:], gm[:, 0:1], ALU.is_ge)
            p.stt(c.scr[:, 8:16], c.scr[:, 0:8], NEG, g8[:], ALU.mult, ALU.add)
            p.red(gm[:, 1:2], c.scr[:, 8:16], op=ALU.max)
            p.tt(gm[:, 3:4], gm[:, 0:1], gm[:, 1:2], op=ALU.add)
            p.ts(gm[:, 3:4], gm[:, 3:4], 1.0e-8, ALU.add)
            p.recip(gm[:, 4:5], gm[:, 3:4])
            p.ts(c.scr[:, 0:8], g8[:], gm[:, 1:2], ALU.is_ge)
            p.tt(c.scr[:, 8:16], g8[:], c.scr[:, 0:8], op=ALU.mult)
            p.ts(cw[:, sl(t, 8)], c.scr[:, 8:16], gm[:, 4:5], ALU.mult,
                 s2=0.5, op1=ALU.mult)

        for e_i in range(NE):
            p.dma(w1.rearrange("p (cc hh) -> p cc hh", cc=NC),
                  st.ffn_W1[e_i].rearrange("(cc p) hh -> p cc hh", p=128),
                  eng="gpsimd")
            p.dma(w2.rearrange("p (hc o) -> p hc o", o=D),
                  st.ffn_W2[e_i].rearrange("(hc p) o -> p hc o", p=128),
                  eng="gpsimd")
            p.dma(b1n[:], st.ffn_b1[e_i].rearrange("(a b) -> a b", a=16))
            p.tr(pms[0:128, 0:16], b1n[:], c.ident_f[0:16, 0:16])
            p.act(b1c[:], pms[:, 0:16], AF.Copy)
            for stile in range(4):
                for hc in range(16):
                    for fc in range(NC):
                        p.mm(pm[:], w1[:, fc * DHID + hc * 128:fc * DHID + (hc + 1) * 128],
                             zT[:, fc * S + stile * 512:fc * S + (stile + 1) * 512],
                             start=(fc == 0), stop=(fc == NC - 1))
                    p.act(h[:, sl(hc, 512)], pm[:], AF.Relu, bias=b1c[:, hc:hc + 1])
                for tk in range(4):
                    t = stile * 4 + tk
                    for half in range(2):
                        for hc in range(16):
                            p.mm(pmo[:],
                                 h[:, hc * 512 + tk * 128:hc * 512 + (tk + 1) * 128],
                                 w2[:, hc * D + half * 512:hc * D + (half + 1) * 512],
                                 start=(hc == 0), stop=(hc == 15))
                        xsl = c.x[:, t * D + half * 512:t * D + (half + 1) * 512]
                        p.stt(xsl, pmo[:], cw[:, t * 8 + e_i:t * 8 + e_i + 1], xsl)

        for t in range(NT):
            p.tr(pms[0:8, 0:128], cw[:, sl(t, 8)], c.ident_f[:])
            p.act(cwT[:], pms[0:8, 0:128], AF.Copy)
            for half in range(2):
                p.mm(pmo[:], cwT[:], b2s[:, sl(half, 512)])
                xsl = c.x[:, t * D + half * 512:t * D + (half + 1) * 512]
                p.tt(xsl, xsl, pmo[:], op=ALU.add)
        p.flush()


def emit_mamba(p, c, st):
    nc = p.nc
    with ExitStack() as ex:
        zhT = [p.sb(ex, "mb_zhT", [128, D], BF16) for _ in range(2)]
        wbt = p.sb(ex, "mb_wbt", [128, NC * 16], BF16)
        wct = p.sb(ex, "mb_wct", [128, NC * 16], BF16)
        wdt = p.sb(ex, "mb_wdt", [128, NC], BF16)
        wnat = p.sb(ex, "mb_wnat", [16, D], F32)
        adl = p.sb(ex, "mb_adl", [128, 128], F32)
        iadl = p.sb(ex, "mb_iadl", [128, 128], F32)
        state = p.sb(ex, "mb_state", [128, 128], F32)
        bmt = p.sb(ex, "mb_bmt", [128, 16], F32)
        cmt = p.sb(ex, "mb_cmt", [128, 16], F32)
        bbb = p.sb(ex, "mb_bbb", [128, 16], F32)
        bcb = p.sb(ex, "mb_bcb", [128, 16], F32)
        bdc = p.sb(ex, "mb_bdc", [128, 1], F32)
        brow = p.sb(ex, "mb_brow", [1, 16], F32)
        dsp = p.sb(ex, "mb_dsp", [128, 6], F32)
        drow = p.sb(ex, "mb_drow", [1, 128], F32)
        dcols = [p.sb(ex, "mb_dcols", [128, 128], F32) for _ in range(2)]
        flat = [p.sb(ex, "mb_flat", [1, 2048], F32) for _ in range(2)]
        bmb = [p.sb(ex, "mb_bmb", [128, 2048], F32) for _ in range(2)]
        cmb = [p.sb(ex, "mb_cmb", [128, 2048], F32) for _ in range(2)]
        abuf = [p.sb(ex, "mb_abuf", [128, 2048], F32) for _ in range(2)]
        ab = [p.sb(ex, "mb_ab", [128, 2048], F32) for _ in range(2)]
        ww = [p.sb(ex, "mb_ww", [128, 2048], F32) for _ in range(2)]
        bw = [p.sb(ex, "mb_bw", [128, 2048], F32) for _ in range(1)]
        sh = [p.sb(ex, "mb_sh", [128, 2048], F32) for _ in range(2)]
        ybt = p.sb(ex, "mb_ybt", [128, 128], F32)
        pm = p.ps(ex, "mb_pm", [128, 512], F32)
        pms = p.ps(ex, "mb_pms", [128, 128], F32)
        pmy = p.ps(ex, "mb_pmy", [128, 128], F32)

        p.dma(adl.rearrange("p (cc n) -> p cc n", n=16),
              st.A.rearrange("(cc p) n -> p cc n", p=128))
        p.recip(iadl[:], adl[:])
        p.memset(state[:], 0.0)
        p.dma(wnat[:], st.W_B)
        for ci in range(NC):
            p.tr(pms[:, 0:16], wnat[:, sl(ci, 128)], c.ident_f[0:16, 0:16])
            p.act(wbt[:, sl(ci, 16)], pms[:, 0:16], AF.Copy)
        p.dma(wnat[:], st.W_C)
        for ci in range(NC):
            p.tr(pms[:, 0:16], wnat[:, sl(ci, 128)], c.ident_f[0:16, 0:16])
            p.act(wct[:, sl(ci, 16)], pms[:, 0:16], AF.Copy)
        p.dma(wnat[0:1, :], st.W_Delta)
        for ci in range(NC):
            p.tr(pms[:, 0:1], wnat[0:1, sl(ci, 128)], c.ident_f[0:1, 0:1])
            p.act(wdt[:, ci:ci + 1], pms[:, 0:1], AF.Copy)
        p.dma(brow[:], st.b_B.rearrange("(a n) -> a n", a=1))
        p.mm(pms[:, 0:16], c.ones_f[0:1, :], brow[:])
        p.act(bbb[:], pms[:, 0:16], AF.Copy)
        p.dma(brow[:], st.b_C.rearrange("(a n) -> a n", a=1))
        p.mm(pms[:, 0:16], c.ones_f[0:1, :], brow[:])
        p.act(bcb[:], pms[:, 0:16], AF.Copy)
        p.dma(brow[0:1, 0:1], st.b_Delta.rearrange("(a n) -> a n", a=1))
        p.dma(brow[0:1, 1:2], st.bias_Delta.rearrange("(a n) -> a n", a=1))
        p.tt(brow[0:1, 2:3], brow[0:1, 0:1], brow[0:1, 1:2], op=ALU.add)
        p.mm(pms[:, 0:1], c.ones_f[0:1, :], brow[0:1, 2:3])
        p.act(bdc[:], pms[:, 0:1], AF.Copy)

        def prep_tile(t):
            tp = t % 2
            ln_tile(p, c, t,
                    bass.AP(tensor=zhT[tp], offset=0,
                            ap=[[D, 128], [128, NC], [1, TT]]))
            for fc in range(NC):
                p.mm(pms[:, 0:16], zhT[tp][:, sl(fc, 128)], wbt[:, sl(fc, 16)],
                     start=(fc == 0), stop=(fc == NC - 1))
            p.tt(bmt[:], pms[:, 0:16], bbb[:], op=ALU.add)
            for fc in range(NC):
                p.mm(pms[:, 0:16], zhT[tp][:, sl(fc, 128)], wct[:, sl(fc, 16)],
                     start=(fc == 0), stop=(fc == NC - 1))
            p.tt(cmt[:], pms[:, 0:16], bcb[:], op=ALU.add)
            for fc in range(NC):
                p.mm(pms[:, 0:1], zhT[tp][:, sl(fc, 128)], wdt[:, fc:fc + 1],
                     start=(fc == 0), stop=(fc == NC - 1))
            p.ts(dsp[:, 0:1], pms[:, 0:1], bdc[:], ALU.add)
            p.act(dsp[:, 1:2], dsp[:, 0:1], AF.Abs)
            p.act(dsp[:, 2:3], dsp[:, 1:2], AF.Exp, scale=-1.0)
            p.act(dsp[:, 3:4], dsp[:, 2:3], AF.Ln, bias=c.onecol[:])
            p.act(dsp[:, 4:5], dsp[:, 0:1], AF.Relu)
            p.tt(dsp[:, 5:6], dsp[:, 3:4], dsp[:, 4:5], op=ALU.add)
            p.tr(pms[0:1, 0:128], dsp[:, 5:6], c.ident_f[:])
            p.act(drow[:], pms[0:1, 0:128], AF.Copy)
            p.mm(pms[:], c.ones_f[0:1, :], drow[:])
            p.act(dcols[tp][:], pms[:], AF.Copy)
            bmflat = bass.AP(tensor=st.bm_scr[tp].tensor, offset=0,
                             ap=[[2048, 1], [1, 2048]])
            cmflat = bass.AP(tensor=st.cm_scr[tp].tensor, offset=0,
                             ap=[[2048, 1], [1, 2048]])
            p.dma(st.bm_scr[tp], bmt[:])
            p.dma(flat[tp][:], bmflat)
            for k in range(4):
                p.mm(pm[:], c.ones_f[0:1, :], flat[tp][:, sl(k, 512)])
                p.act(bmb[tp][:, sl(k, 512)], pm[:], AF.Copy)
            p.dma(st.cm_scr[tp], cmt[:])
            p.dma(flat[tp][:], cmflat)
            for k in range(4):
                p.mm(pm[:], c.ones_f[0:1, :], flat[tp][:, sl(k, 512)])
                p.act(cmb[tp][:, sl(k, 512)], pm[:], AF.Copy)

        def unit_pre(t, ci):
            tp, b = t % 2, ci % 2
            v3 = [128, TT, 16]
            p.tt(abuf[b].rearrange("p (t n) -> p t n", n=16),
                 dcols[tp][:, :, None].broadcast_to(v3),
                 adl[:, sl(ci, 16)][:, None, :].broadcast_to(v3), eng="gpsimd")
            p.tt(ww[b].rearrange("p (t n) -> p t n", n=16),
                 bmb[tp].rearrange("p (t n) -> p t n", n=16),
                 zhT[tp][:, sl(ci, 128)][:, :, None].broadcast_to(v3),
                 eng="gpsimd")
            p.act(ab[b][:], abuf[b][:], AF.Exp)
            p.stt(bw[0].rearrange("p (t n) -> p t n", n=16),
                  ab[b].rearrange("p (t n) -> p t n", n=16), -1.0,
                  iadl[:, sl(ci, 16)][:, None, :].broadcast_to(v3),
                  ALU.add, ALU.mult)
            p.tt(bw[0][:], bw[0][:], ww[b][:])

        def unit_scans(t, ci):
            tp, b = t % 2, ci % 2
            for n in range(16):
                p.scan(bass.AP(tensor=sh[b], offset=n, ap=[[2048, 128], [16, TT]]),
                       bass.AP(tensor=ab[b], offset=n, ap=[[2048, 128], [16, TT]]),
                       bass.AP(tensor=bw[0], offset=n, ap=[[2048, 128], [16, TT]]),
                       state[:, ci * 16 + n:ci * 16 + n + 1])
            p.copy(state[:, sl(ci, 16)], sh[b][:, 2032:2048])

        def unit_tail(t, ci):
            tp, b = t % 2, ci % 2
            p.tt(abuf[b][:], sh[b][:], cmb[tp][:], eng="gpsimd")
            p.red(ybt[:], abuf[b].rearrange("p (t n) -> p t n", n=16))
            p.tr(pmy[:], ybt[:], c.ident_f[:])
            xsl = c.x[:, t * D + ci * 128:t * D + (ci + 1) * 128]
            p.tt(xsl, xsl, pmy[:], op=ALU.add)

        units = [(t, ci) for t in range(NT) for ci in range(NC)]
        prep_tile(0)
        for i, (t, ci) in enumerate(units):
            if ci == 0 and t + 1 < NT:
                prep_tile(t + 1)
            unit_pre(t, ci)
            if i >= 1:
                unit_tail(*units[i - 1])
            unit_scans(t, ci)
        unit_tail(*units[-1])
        p.flush()


def emit_l1ffn_prep(p, c, st):
    nc = p.nc
    with ExitStack() as ex:
        ldn = p.sb(ex, "lp_ldn", [4, 100], F32)
        ldT = p.sb(ex, "lp_ldT", [100, 4], F32)
        dax = p.sb(ex, "lp_dax", [100, D], F32)
        lun = p.sb(ex, "lp_lun", [50, 4], F32)
        luT = p.sb(ex, "lp_luT", [4, 50], F32)
        m1 = p.sb(ex, "lp_m1", [4, D], F32)
        m2 = p.sb(ex, "lp_m2", [50, D], F32)
        upn = p.sb(ex, "lp_upn", [128, NC * 50], F32)
        upT = p.sb(ex, "lp_upT", [50, D], F32)
        wsb = p.sb(ex, "lp_wsb", [128, NC * D], F32)
        wet = p.sb(ex, "lp_wet", [128, NC * D], BF16)
        wtt = p.sb(ex, "lp_wtt", [128, 128], F32)
        brw = p.sb(ex, "lp_brw", [1, D], F32)
        hbs = p.sb(ex, "lp_hbs", [128, D], F32)
        pm = p.ps(ex, "lp_pm", [128, 512], F32)
        pmt = p.ps(ex, "lp_pmt", [128, 128], F32)

        p.dma(ldn[:], st.lora_down)
        p.dma(dax[:], st.down_aux)
        p.dma(lun[:], st.lora_up)
        p.dma(upn.rearrange("p (cc u) -> p cc u", u=50),
              st.up_aux.rearrange("(cc p) u -> p cc u", p=128))
        p.dma(wsb.rearrange("p (co i) -> p co i", i=D),
              st.W.rearrange("(co p) i -> p co i", p=128))
        p.dma(brw[:], st.b.rearrange("(a n) -> a n", a=1))

        p.tr(pmt[0:100, 0:4], ldn[:], c.ident_f[0:4, 0:4])
        p.act(ldT[:], pmt[0:100, 0:4], AF.Copy)
        p.tr(pmt[0:4, 0:50], lun[:], c.ident_f[0:50, 0:50])
        p.act(luT[:], pmt[0:4, 0:50], AF.Copy)
        for half in range(2):
            p.mm(pm[0:4, :], ldT[:], dax[:, sl(half, 512)])
            p.act(m1[:, sl(half, 512)], pm[0:4, :], AF.Copy)
        for half in range(2):
            p.mm(pm[0:50, :], luT[:], m1[:, sl(half, 512)])
            p.act(m2[:, sl(half, 512)], pm[0:50, :], AF.Copy)
        for ci in range(NC):
            p.tr(pmt[0:50, :], upn[:, sl(ci, 50)], c.ident_f[:])
            p.act(upT[:, sl(ci, 128)], pmt[0:50, 0:128], AF.Copy)
        for ci in range(NC):
            for half in range(2):
                p.mm(pm[:], m2[:, sl(ci, 128)], upT[:, sl(half, 512)])
                for cq in range(4):
                    co = half * 4 + cq
                    p.tr(pmt[:], wsb[:, co * D + ci * 128:co * D + (ci + 1) * 128],
                         c.ident_f[:])
                    p.act(wtt[:], pmt[:], AF.Copy, scale=0.5)
                    p.stt(wet[:, ci * D + co * 128:ci * D + (co + 1) * 128],
                          pm[:, sl(cq, 128)], 0.5, wtt[:], ALU.mult, ALU.add)
        for half in range(2):
            p.mm(pm[:], c.ones_f[0:1, :], brw[:, sl(half, 512)])
            p.act(hbs[:, sl(half, 512)], pm[:], AF.Copy, scale=0.5)
        p.dma(st.wet_scr, wet[:])
        p.dma(st.hb_scr, hbs[:])
        p.flush()


def emit_l1ffn(p, c, st):
    nc = p.nc
    with ExitStack() as ex:
        zhT = p.sb(ex, "lf_zhT", [128, D], BF16)
        wet = p.sb(ex, "lf_wet", [128, NC * D], BF16)
        hbs = p.sb(ex, "lf_hbs", [128, D], F32)
        tmp = p.sb(ex, "lf_tmp", [128, 512], F32)
        pm = p.ps(ex, "lf_pm", [128, 512], F32)
        p.dma(wet[:], st.wet_scr)
        p.dma(hbs[:], st.hb_scr)
        for t in range(NT):
            ln_tile(p, c, t,
                    bass.AP(tensor=zhT, offset=0, ap=[[D, 128], [128, NC], [1, TT]]))
            for half in range(2):
                for ci in range(NC):
                    p.mm(pm[:], zhT[:, sl(ci, 128)],
                         wet[:, ci * D + half * 512:ci * D + (half + 1) * 512],
                         start=(ci == 0), stop=(ci == NC - 1))
                p.tt(tmp[:], pm[:], hbs[:, sl(half, 512)], op=ALU.add)
                xsl = c.x[:, t * D + half * 512:t * D + (half + 1) * 512]
                p.tt(xsl, xsl, tmp[:], op=ALU.add)
        p.flush()


def emit_attn(p, c, st):
    nc = p.nc
    with ExitStack() as ex:
        cqT = p.sb(ex, "at_cqT", [128, 2 * S], BF16)
        ckT = p.sb(ex, "at_ckT", [128, 2 * S], BF16)
        qrr = p.sb(ex, "at_qrr", [128, NT * 512], BF16)
        vsb = p.sb(ex, "at_vsb", [128, NT * 1040], BF16)
        krT = p.sb(ex, "at_krT", [32, S], BF16)
        wuq = p.sb(ex, "at_wuq", [128, 2 * D], BF16)
        wuk = p.sb(ex, "at_wuk", [128, 2 * D], BF16)
        wqr = p.sb(ex, "at_wqr", [128, 2 * 512], BF16)
        wo = p.sb(ex, "at_wo", [128, NC * D], BF16)

        p.dma(wuq.rearrange("p (cc f) -> p cc f", f=D),
              st.W_UQ.rearrange("(cc p) f -> p cc f", p=128), eng="gpsimd")
        p.dma(wuk.rearrange("p (cc f) -> p cc f", f=D),
              st.W_UK.rearrange("(cc p) f -> p cc f", p=128), eng="gpsimd")
        p.dma(wqr.rearrange("p (cc f) -> p cc f", f=512),
              st.W_QR.rearrange("(cc p) f -> p cc f", p=128), eng="gpsimd")
        p.dma(wo.rearrange("p (cc f) -> p cc f", f=D),
              st.W_O.rearrange("(cc p) f -> p cc f", p=128), eng="gpsimd")

        with ExitStack() as ex1:
            zhT = p.sb(ex1, "at_zhT", [128, D], BF16)
            wdq = p.sb(ex1, "at_wdq", [128, NC * DL], BF16)
            wdk = p.sb(ex1, "at_wdk", [128, NC * DL], BF16)
            wuv = p.sb(ex1, "at_wuv", [128, 2 * D], BF16)
            wkr = p.sb(ex1, "at_wkr", [128, NC * 32], BF16)
            cost = p.sb(ex1, "at_cos", [128, 16], F32)
            sint = p.sb(ex1, "at_sin", [128, 16], F32)
            rt1 = p.sb(ex1, "at_rt1", [128, 256], F32)
            rt2 = p.sb(ex1, "at_rt2", [128, 256], F32)
            krr = p.sb(ex1, "at_krr", [128, 32], BF16)
            pm = p.ps(ex1, "at_pm", [128, 512], F32)
            pmb = p.ps(ex1, "at_pmb", [128, 512], BF16)

            p.dma(wdq.rearrange("p (cc f) -> p cc f", f=DL),
                  st.W_DQ.rearrange("(cc p) f -> p cc f", p=128), eng="gpsimd")
            p.dma(wdk.rearrange("p (cc f) -> p cc f", f=DL),
                  st.W_DKV.rearrange("(cc p) f -> p cc f", p=128), eng="gpsimd")
            p.dma(wuv.rearrange("p (cc f) -> p cc f", f=D),
                  st.W_UV.rearrange("(cc p) f -> p cc f", p=128), eng="gpsimd")
            p.dma(wkr.rearrange("p (cc f) -> p cc f", f=32),
                  st.W_KR.rearrange("(cc p) f -> p cc f", p=128), eng="gpsimd")

            for t in range(NT):
                ln_tile(p, c, t,
                        bass.AP(tensor=zhT, offset=0, ap=[[D, 128], [128, NC], [1, TT]]))
                p.dma(cost[:], st.cos_t[sl(t, 128), :])
                p.dma(sint[:], st.sin_t[sl(t, 128), :])
                for lc in range(2):
                    for fc in range(NC):
                        p.mm(pm[:, 0:128],
                             wdq[:, fc * DL + lc * 128:fc * DL + (lc + 1) * 128],
                             zhT[:, sl(fc, 128)],
                             start=(fc == 0), stop=(fc == NC - 1))
                    p.act(cqT[:, lc * S + t * TT:lc * S + (t + 1) * TT],
                          pm[:, 0:128], AF.Copy)
                    for fc in range(NC):
                        p.mm(pm[:, 0:128],
                             wdk[:, fc * DL + lc * 128:fc * DL + (lc + 1) * 128],
                             zhT[:, sl(fc, 128)],
                             start=(fc == 0), stop=(fc == NC - 1))
                    p.act(ckT[:, lc * S + t * TT:lc * S + (t + 1) * TT],
                          pm[:, 0:128], AF.Copy)
                for fc in range(NC):
                    p.mm(pm[:, 0:32], zhT[:, sl(fc, 128)], wkr[:, sl(fc, 32)],
                         start=(fc == 0), stop=(fc == NC - 1))
                pe = pm[:, 0:32].rearrange("p (f two) -> p f two", two=2)
                p.tt(rt1[:, 0:16], pe[:, :, 0], cost[:])
                p.tt(rt2[:, 0:16], pe[:, :, 1], sint[:])
                p.tt(krr.rearrange("p (f two) -> p f two", two=2)[:, :, 0],
                     rt1[:, 0:16], rt2[:, 0:16], op=ALU.subtract)
                p.tt(rt1[:, 0:16], pe[:, :, 0], sint[:])
                p.tt(rt2[:, 0:16], pe[:, :, 1], cost[:])
                p.tt(krr.rearrange("p (f two) -> p f two", two=2)[:, :, 1],
                     rt1[:, 0:16], rt2[:, 0:16], op=ALU.add)
                p.tr(pmb[0:32, 0:128], krr[:], c.ident_b[:])
                p.act(krT[:, sl(t, 128)], pmb[0:32, 0:128], AF.Copy)
                for lc in range(2):
                    p.mm(pm[:], cqT[:, lc * S + t * TT:lc * S + (t + 1) * TT],
                         wqr[:, sl(lc, 512)],
                         start=(lc == 0), stop=(lc == 1))
                pe = pm.rearrange("p (h f two) -> p h f two", two=2, f=16)
                cb = cost[:, None, :].broadcast_to([128, 16, 16])
                sb = sint[:, None, :].broadcast_to([128, 16, 16])
                r3 = rt1.rearrange("p (h f) -> p h f", f=16)
                r4 = rt2.rearrange("p (h f) -> p h f", f=16)
                qv = qrr[:, sl(t, 512)].rearrange("p (h f two) -> p h f two", two=2, f=16)
                p.tt(r3, pe[:, :, :, 0], cb)
                p.tt(r4, pe[:, :, :, 1], sb)
                p.tt(qv[:, :, :, 0], r3, r4, op=ALU.subtract)
                p.tt(r3, pe[:, :, :, 0], sb)
                p.tt(r4, pe[:, :, :, 1], cb)
                p.tt(qv[:, :, :, 1], r3, r4, op=ALU.add)
                for half in range(2):
                    for lc in range(2):
                        p.mm(pm[:], ckT[:, lc * S + t * TT:lc * S + (t + 1) * TT],
                             wuv[:, lc * D + half * 512:lc * D + (half + 1) * 512],
                             start=(lc == 0), stop=(lc == 1))
                    dst = bass.AP(tensor=vsb, offset=t * 1040 + half * 520,
                                  ap=[[NT * 1040, 128], [65, 8], [1, 64]])
                    p.act(dst, pm.rearrange("p (h f) -> p h f", f=64), AF.Copy)
                p.memset(bass.AP(tensor=vsb, offset=t * 1040 + 64,
                                 ap=[[NT * 1040, 128], [65, 16]]), 1.0)
            p.flush()

        with ExitStack() as ex2:
            atT = p.sb(ex2, "at_atT", [128, S], BF16)
            qh = p.sb(ex2, "at_qh", [64, S], BF16)
            kh = p.sb(ex2, "at_kh", [64, S], BF16)
            qrh = p.sb(ex2, "at_qrh", [32, S], BF16)
            maskw = p.sb(ex2, "at_maskw", [128, 4 * 512], F32)
            scb = [p.sb(ex2, "at_scb", [128, 512], F32) for _ in range(3)]
            etb = [p.sb(ex2, "at_etb", [128, 512], BF16) for _ in range(3)]
            rrs = p.sb(ex2, "at_rrs", [1, 512], F32)
            rsum = p.sb(ex2, "at_rsum", [1, 512], F32)
            rbs = p.sb(ex2, "at_rbs", [64, 512], F32)
            psc = [p.ps(ex2, "at_psc", [128, 512], F32) for _ in range(3)]
            pmav = p.ps(ex2, "at_pmav", [65, 512], F32)
            pm2 = p.ps(ex2, "at_pm2", [128, 1024], F32)
            pmr = p.ps(ex2, "at_pmr", [64, 512], F32)

            p.dma(maskw[:], st.mask512)

            ATT_LVL = int(os.environ.get("ATT_LVL", "9"))
            for h_i in range(NH):
                hc, hp = h_i // 2, h_i % 2
                for qc in range(4):
                    for lc in range(2):
                        p.mm(pm2[0:64, 0:512],
                             wuq[:, lc * D + h_i * 64:lc * D + (h_i + 1) * 64],
                             cqT[:, lc * S + qc * 512:lc * S + (qc + 1) * 512],
                             start=(lc == 0), stop=(lc == 1))
                    p.act(qh[:, sl(qc, 512)], pm2[0:64, 0:512], AF.Copy)
                    for lc in range(2):
                        p.mm(pm2[0:64, 512:1024],
                             wuk[:, lc * D + h_i * 64:lc * D + (h_i + 1) * 64],
                             ckT[:, lc * S + qc * 512:lc * S + (qc + 1) * 512],
                             start=(lc == 0), stop=(lc == 1))
                    p.act(kh[:, sl(qc, 512)], pm2[0:64, 512:1024], AF.Copy)
                for qc in range(4):
                    for tk in range(4):
                        t = qc * 4 + tk
                        p.tr(c.pm_trb[0:32, sl(tk, 128)],
                             qrr[:, t * 512 + h_i * 32:t * 512 + (h_i + 1) * 32],
                             c.ident_b[:])
                    p.act(qrh[:, sl(qc, 512)], c.pm_trb[0:32, 0:512], AF.Copy)

                if ATT_LVL < 1:
                    continue
                for qs in range(4):
                    nkt = 4 * qs + 4
                    qsl = slice(qs * 512, (qs + 1) * 512)

                    def e_score(kt):
                        b = kt % 3
                        p.mm(psc[b][:], kh[:, sl(kt, 128)], qh[:, qsl],
                             start=True, stop=False)
                        p.mm(psc[b][:], krT[:, sl(kt, 128)], qrh[:, qsl],
                             start=False, stop=True)

                    def e_sm(kt):
                        b = kt % 3
                        p.ts(scb[b][:], psc[b][:], 80.0, ALU.min,
                             s2=-80.0, op1=ALU.max)
                        if kt >= 4 * qs:
                            r = kt - 4 * qs
                            p.tt(scb[b][:], scb[b][:],
                                 maskw[:, sl(r, 512)], op=ALU.add)
                        p.act(etb[b][:], scb[b][:], AF.Exp, scale=INV_SQRT_DQK)

                    def e_av(kt):
                        b = kt % 3
                        p.mm(pmav[:],
                             vsb[:, kt * 1040 + h_i * 65:kt * 1040 + (h_i + 1) * 65],
                             etb[b][:], start=(kt == 0), stop=(kt == nkt - 1))

                    for kt in range(nkt):
                        e_score(kt)
                        if ATT_LVL >= 2:
                            e_sm(kt)
                        if ATT_LVL >= 3 and kt >= 2:
                            e_av(kt - 2)
                    if ATT_LVL < 3:
                        continue
                    e_av(nkt - 2)
                    e_av(nkt - 1)
                    if ATT_LVL < 4:
                        continue
                    p.act(rsum[:], pmav[64:65, :], AF.Copy)
                    p.recip(rrs[:], rsum[:])
                    p.mm(pmr[:], c.ones_f[0:1, 0:64], rrs[:])
                    p.act(rbs[:], pmr[:], AF.Copy)
                    p.tt(atT[hp * 64:(hp + 1) * 64, qsl],
                         pmav[0:64, :], rbs[:], op=ALU.mult)
                if hp == 1 and ATT_LVL >= 5:
                    for t in range(NT):
                        for half in range(2):
                            hx = half * 512
                            p.mm(pm2[:, hx:hx + 512], atT[:, sl(t, TT)],
                                 wo[:, hc * D + hx:hc * D + hx + 512])
                        for half in range(2):
                            hx = half * 512
                            xsl = c.x[:, t * D + hx:t * D + hx + 512]
                            p.tt(xsl, xsl, pm2[:, hx:hx + 512], op=ALU.add)
            p.flush()


class Ctx:
    pass


def build_nc(parts=ALL_PARTS):
    nc = bass.Bass("TRN2", target_bir_lowering=False)
    st = Ctx()
    st.x_in = nc.dram_tensor("x_in", [S, D], F32, kind="ExternalInput").ap()
    st.out = nc.dram_tensor("out", [S, D], F32, kind="ExternalOutput").ap()
    st.ident_in = nc.dram_tensor("ident_in", [128, 128], F32, kind="ExternalInput").ap()
    st.mask_in = nc.dram_tensor("mask_in", [128, 128], F32, kind="ExternalInput").ap()
    st.mask512 = nc.dram_tensor("mask512", [128, 4 * 512], F32, kind="ExternalInput").ap()
    st.cos_t = nc.dram_tensor("cos_t", [S, 16], F32, kind="ExternalInput").ap()
    st.sin_t = nc.dram_tensor("sin_t", [S, 16], F32, kind="ExternalInput").ap()
    st.A = nc.dram_tensor("L0_attn_A", [D, 16], F32, kind="ExternalInput").ap()
    st.W_B = nc.dram_tensor("L0_attn_W_B", [16, D], F32, kind="ExternalInput").ap()
    st.b_B = nc.dram_tensor("L0_attn_b_B", [16], F32, kind="ExternalInput").ap()
    st.W_C = nc.dram_tensor("L0_attn_W_C", [16, D], F32, kind="ExternalInput").ap()
    st.b_C = nc.dram_tensor("L0_attn_b_C", [16], F32, kind="ExternalInput").ap()
    st.W_Delta = nc.dram_tensor("L0_attn_W_Delta", [1, D], F32, kind="ExternalInput").ap()
    st.b_Delta = nc.dram_tensor("L0_attn_b_Delta", [1], F32, kind="ExternalInput").ap()
    st.bias_Delta = nc.dram_tensor("L0_attn_bias_Delta", [1], F32, kind="ExternalInput").ap()
    st.gate_W = nc.dram_tensor("L0_ffn_gate_W", [D, 8], F32, kind="ExternalInput").ap()
    st.ffn_W1 = nc.dram_tensor("L0_ffn_W1", [NE, D, DHID], F32, kind="ExternalInput").ap()
    st.ffn_b1 = nc.dram_tensor("L0_ffn_b1", [NE, DHID], F32, kind="ExternalInput").ap()
    st.ffn_W2 = nc.dram_tensor("L0_ffn_W2", [NE, DHID, D], F32, kind="ExternalInput").ap()
    st.ffn_b2 = nc.dram_tensor("L0_ffn_b2", [NE, D], F32, kind="ExternalInput").ap()
    st.W_DQ = nc.dram_tensor("L1_attn_W_DQ", [D, DL], F32, kind="ExternalInput").ap()
    st.W_UQ = nc.dram_tensor("L1_attn_W_UQ", [DL, D], F32, kind="ExternalInput").ap()
    st.W_QR = nc.dram_tensor("L1_attn_W_QR", [DL, 512], F32, kind="ExternalInput").ap()
    st.W_DKV = nc.dram_tensor("L1_attn_W_DKV", [D, DL], F32, kind="ExternalInput").ap()
    st.W_UK = nc.dram_tensor("L1_attn_W_UK", [DL, D], F32, kind="ExternalInput").ap()
    st.W_UV = nc.dram_tensor("L1_attn_W_UV", [DL, D], F32, kind="ExternalInput").ap()
    st.W_KR = nc.dram_tensor("L1_attn_W_KR", [D, 32], F32, kind="ExternalInput").ap()
    st.W_O = nc.dram_tensor("L1_attn_W_O", [D, D], F32, kind="ExternalInput").ap()
    st.W = nc.dram_tensor("L1_ffn_W", [D, D], F32, kind="ExternalInput").ap()
    st.b = nc.dram_tensor("L1_ffn_b", [D], F32, kind="ExternalInput").ap()
    st.lora_down = nc.dram_tensor("L1_ffn_lora_down", [4, 100], F32, kind="ExternalInput").ap()
    st.lora_up = nc.dram_tensor("L1_ffn_lora_up", [50, 4], F32, kind="ExternalInput").ap()
    st.down_aux = nc.dram_tensor("L1_ffn_down_aux", [100, D], F32, kind="ExternalInput").ap()
    st.up_aux = nc.dram_tensor("L1_ffn_up_aux", [D, 50], F32, kind="ExternalInput").ap()
    st.bm_scr = [nc.dram_tensor(f"bm_scr{i}", [128, 16], F32, kind="Internal").ap()
                 for i in range(2)]
    st.cm_scr = [nc.dram_tensor(f"cm_scr{i}", [128, 16], F32, kind="Internal").ap()
                 for i in range(2)]
    st.wet_scr = nc.dram_tensor("wet_scr", [128, NC * D], BF16, kind="Internal").ap()
    st.hb_scr = nc.dram_tensor("hb_scr", [128, D], F32, kind="Internal").ap()

    c = Ctx()
    with ExitStack() as top:
        semnames = ("tensor", "vector", "scalar", "gpsimd",
                    "dsync", "dgp", "dscal", "dvec")
        sems = {k: top.enter_context(nc.semaphore("sem_" + k)) for k in semnames}
        p = Prog(nc, sems)
        c.x = p.sb(top, "x_sb", [128, NT * D], F32)
        c.ident_f = p.sb(top, "ident_f", [128, 128], F32)
        c.ident_b = p.sb(top, "ident_b", [128, 128], BF16)
        c.maskc = p.sb(top, "maskc", [128, 128], F32)
        c.ones_f = p.sb(top, "ones_f", [1, 128], F32)
        c.zhat = p.sb(top, "zhat", [128, D], BF16)
        c.scr = p.sb(top, "ln_scr", [128, D], F32)
        c.sumc = p.sb(top, "ln_sum", [128, 1], F32)
        c.sqc = p.sb(top, "ln_sq", [128, 1], F32)
        c.mc = p.sb(top, "ln_m", [128, 1], F32)
        c.msq = p.sb(top, "ln_msq", [128, 1], F32)
        c.varc = p.sb(top, "ln_var", [128, 1], F32)
        c.lnc = p.sb(top, "ln_ln", [128, 1], F32)
        c.rstdc = p.sb(top, "ln_rstd", [128, 1], F32)
        c.epscol = p.sb(top, "epscol", [128, 1], F32)
        c.onecol = p.sb(top, "onecol", [128, 1], F32)
        c.pm_trb = p.ps(top, "pm_trb", [128, D], BF16)
        if True:
            p.dma(c.x.rearrange("p (n d) -> p n d", d=D),
                  st.x_in.rearrange("(n p) d -> p n d", p=128))
            p.dma(c.ident_f[:], st.ident_in)
            p.dma(c.ident_b[:], st.ident_in, eng="gpsimd")
            p.dma(c.maskc[:], st.mask_in)
            p.memset(c.ones_f[:], 1.0)
            p.memset(c.epscol[:], 1.0e-5)
            p.memset(c.onecol[:], 1.0)
            p.flush()
            for part in parts:
                if part in ("moe1", "moe2"):
                    emit_moe(p, c, st)
                elif part == "mamba":
                    emit_mamba(p, c, st)
                elif part == "ffn1":
                    emit_l1ffn_prep(p, c, st)
                    emit_l1ffn(p, c, st)
                elif part == "ffn1prep":
                    emit_l1ffn_prep(p, c, st)
                elif part == "ffn2":
                    emit_l1ffn(p, c, st)
                elif part == "attn":
                    emit_attn(p, c, st)
            p.dma(st.out.rearrange("(n p) d -> p n d", p=128),
                  c.x.rearrange("p (n d) -> p n d", d=D))
            p.flush(final=True)
    return nc


def host_consts():
    ident = np.eye(128, dtype=np.float32)
    mask = np.where(np.arange(128)[:, None] <= np.arange(128)[None, :],
                    0.0, NEG).astype(np.float32)
    inv = (10000.0 ** (-np.arange(0, 32, 2, dtype=np.float64) / 32.0))
    ang = np.arange(S, dtype=np.float64)[:, None] * inv[None, :]
    cos_t = np.cos(ang).astype(np.float32)
    sin_t = np.sin(ang).astype(np.float32)
    kk = np.arange(128)[:, None]
    qq = np.arange(512)[None, :]
    m512 = np.concatenate(
        [np.where(qq >= kk + 128 * r, 0.0, NEG) for r in range(4)],
        axis=1).astype(np.float32)
    return {"ident_in": ident, "mask_in": mask, "cos_t": cos_t, "sin_t": sin_t,
            "mask512": m512}


def make_in_maps(inputs, xfull):
    w = {k: np.ascontiguousarray(np.asarray(v, dtype=np.float32))
         for k, v in inputs.items() if k != "L0_attn_sequence"}
    consts = host_consts()
    maps = []
    for b in range(8):
        m = {"x_in": np.ascontiguousarray(xfull[b])}
        m.update(w)
        m.update(consts)
        maps.append(m)
    return maps


def kernel(**inputs):
    xfull = np.ascontiguousarray(np.asarray(inputs["L0_attn_sequence"], np.float32))
    nc = build_nc(ALL_PARTS)
    maps = make_in_maps(inputs, xfull)
    res = run_bass_kernel_spmd(nc, maps, core_ids=list(range(8)))
    return np.stack([res.results[b]["out"] for b in range(8)], axis=0)


# version 16 (final)
# speedup vs baseline: 1.3554x; 1.3534x over previous
import math
import os
from contextlib import ExitStack

KSER = int(os.environ.get("KSER", "0"))

import numpy as np
import ml_dtypes

import concourse.bass as bass
import concourse.mybir as mybir
from concourse.bass_utils import run_bass_kernel_spmd

F32 = mybir.dt.float32
BF16 = mybir.dt.bfloat16
AF = mybir.ActivationFunctionType
ALU = mybir.AluOpType
AX = mybir.AxisListType

S, D, TT = 2048, 1024, 128
NT = S // TT
NC = D // 128
NH, DH, DHR, DL = 16, 64, 32, 256
NE, DHID = 8, 2048
INV_SQRT_DQK = 1.0 / math.sqrt(96.0)
NEG = -1.0e30

ALL_PARTS = ("moe1", "mamba", "moe2", "ffn1", "attn", "ffn2")

_UID = [0]


def _nm(s):
    _UID[0] += 1
    return f"{s}_{_UID[0]}"


class Op:
    __slots__ = ("fn", "deps", "dma", "dma_idx", "need_inc", "mval")

    def __init__(self, fn, dma, dma_idx):
        self.fn = fn
        self.deps = set()
        self.dma = dma
        self.dma_idx = dma_idx
        self.need_inc = False
        self.mval = 0


class Prog:
    CENG = ("tensor", "vector", "scalar", "gpsimd")
    DSEM = {"sync": "dsync", "gpsimd": "dgp", "scalar": "dscal", "vector": "dvec"}
    QUEUES = ("sync", "tensor", "vector", "scalar", "gpsimd")

    def __init__(self, nc, sems):
        self.nc = nc
        self.sems = sems
        self.ops = {k: [] for k in self.QUEUES}
        self.ndma = {k: 0 for k in self.QUEUES}
        self.res = {}
        self.addr = {}
        self.rowel = {}

    def _keys(self, ap):
        t = ap.tensor
        nm = t.name
        if nm not in self.addr:
            return [("D", nm)]
        esz = mybir.dt.size(ap.dtype)
        row = self.rowel[nm]
        off = ap.offset % row
        ext = 0
        for s_, n_ in list(ap.ap)[1:]:
            if n_ > 1 and s_ > 0:
                ext += (n_ - 1) * s_
        lo = self.addr[nm] + off * esz
        hi = lo + ext * esz + esz
        if getattr(t, "space", None) is not None and "PSUM" in str(t.space).upper():
            return [("P", b) for b in range(lo // 2048, (hi - 1) // 2048 + 1)]
        return [("S", g) for g in range(lo // 512, (hi - 1) // 512 + 1)]

    def emit(self, eng, fn, reads=(), writes=(), dma=False):
        ops = self.ops[eng]
        idx = len(ops)
        if dma:
            op = Op(fn, True, self.ndma[eng])
            self.ndma[eng] += 1
        else:
            op = Op(fn, False, 0)
        tok = (eng, idx)
        if KSER:
            for q in self.QUEUES:
                if self.ops[q] and not (q == eng):
                    op.deps.add((q, len(self.ops[q]) - 1))
            if idx:
                op.deps.add((eng, idx - 1))

        def add(t_):
            if t_ is not None and t_ != tok:
                op.deps.add(t_)

        rkeys = [k for ap in reads if ap is not None for k in self._keys(ap)]
        wkeys = [k for ap in writes if ap is not None for k in self._keys(ap)]
        for k in rkeys:
            e = self.res.get(k)
            if e:
                add(e["w"])
        for k in wkeys:
            e = self.res.get(k)
            if e:
                add(e["w"])
                for tkn in e["r"]:
                    add(tkn)
        ops.append(op)
        for k in rkeys:
            self.res.setdefault(k, {"w": None, "r": []})["r"].append(tok)
        for k in wkeys:
            self.res[k] = {"w": tok, "r": [tok]} if k in rkeys else {"w": tok, "r": []}
        return tok

    def _wait_needed(self, cq, pq, pop):
        if pop.dma:
            return True
        if KSER:
            return True
        if pq == cq and pq == "tensor":
            return False
        return True

    def flush(self, final=False):
        if not final:
            return
        ops = self.ops
        for cq in self.QUEUES:
            for cop in ops[cq]:
                for (pq, pi) in cop.deps:
                    pop = ops[pq][pi]
                    if not pop.dma and self._wait_needed(cq, pq, pop):
                        pop.need_inc = True
        for q in self.QUEUES:
            for op_ in reversed(ops[q]):
                if not op_.dma:
                    op_.need_inc = True
                    break
        for q in self.QUEUES:
            m = 0
            for op_ in ops[q]:
                if (not op_.dma) and op_.need_inc:
                    m += 1
                op_.mval = m
        sched = {q: [] for q in self.QUEUES}
        last_wait = {}
        for q in self.QUEUES:
            for op_ in ops[q]:
                deps = {}
                if op_.dma and op_.dma_idx:
                    deps[self.DSEM[q]] = 16 * op_.dma_idx
                for (pq, pi) in op_.deps:
                    pop = ops[pq][pi]
                    if not self._wait_needed(q, pq, pop):
                        continue
                    if pop.dma:
                        sn, v = self.DSEM[pq], 16 * (pop.dma_idx + 1)
                    else:
                        sn, v = pq, pop.mval
                        assert pop.need_inc and v > 0
                    if deps.get(sn, 0) < v:
                        deps[sn] = v
                waits = []
                for sn, v in sorted(deps.items()):
                    lw = last_wait.get((q, sn), 0)
                    if v <= lw:
                        continue
                    waits.append((self.sems[sn], v))
                    last_wait[(q, sn)] = v
                if op_.dma:
                    inc = (self.sems[self.DSEM[q]], 16)
                elif op_.need_inc:
                    inc = (self.sems[q], 1)
                else:
                    inc = None
                sched[q].append((waits, op_.fn, inc))
        fwaits = []
        for q in self.QUEUES:
            if ops[q]:
                mv = ops[q][-1].mval
                if mv:
                    fwaits.append((self.sems[q], mv))
            if self.ndma[q]:
                fwaits.append((self.sems[self.DSEM[q]], 16 * self.ndma[q]))
        sched["sync"].append((fwaits, None, None))

        def runner(eng_name, lst):
            def run(e):
                for waits, fn, inc in lst:
                    for sh, v in waits:
                        e.wait_ge(sh, v)
                    if fn is not None:
                        if inc is None:
                            fn(e)
                        else:
                            fn(e).then_inc(inc[0], inc[1])
            return run

        with self.nc.Block() as block:
            for eng in self.QUEUES:
                if sched[eng]:
                    getattr(block, eng)(runner(eng, sched[eng]))

    @staticmethod
    def _isap(x):
        return hasattr(x, "tensor")

    def dma(self, out, in_, eng="sync"):
        self.emit(eng, lambda e: e.dma_start(out, in_),
                  reads=[in_], writes=[out], dma=True)

    def mm(self, out, lhsT, rhs, start=True, stop=True):
        rd = [lhsT, rhs] + ([out] if not start else [])
        self.emit("tensor",
                  lambda e: e.matmul(out, lhsT, rhs, start=start, stop=stop),
                  reads=rd, writes=[out])

    def tr(self, out, in_, ident):
        self.emit("tensor", lambda e: e.transpose(out, in_, ident),
                  reads=[in_, ident], writes=[out])

    def tt(self, out, a, b, op=ALU.mult, eng="vector"):
        self.emit(eng, lambda e: e.tensor_tensor(out=out, in0=a, in1=b, op=op),
                  reads=[a, b], writes=[out])

    def ts(self, out, in0, s1, op0, s2=None, op1=ALU.mult, accum_out=None, eng="vector"):
        rd = [in0] + [x for x in (s1, s2) if self._isap(x)]
        wr = [out] + ([accum_out] if accum_out is not None else [])
        if s2 is None and accum_out is None:
            self.emit(eng, lambda e: e.tensor_scalar(
                out=out, in0=in0, scalar1=s1, scalar2=None, op0=op0),
                reads=rd, writes=wr)
        else:
            self.emit(eng, lambda e: e.tensor_scalar(
                out=out, in0=in0, scalar1=s1, scalar2=s2, op0=op0, op1=op1,
                accum_out=accum_out), reads=rd, writes=wr)

    def stt(self, out, in0, scalar, in1, op0=ALU.mult, op1=ALU.add, eng="vector"):
        rd = [in0, in1] + ([scalar] if self._isap(scalar) else [])
        self.emit(eng, lambda e: e.scalar_tensor_tensor(
            out=out, in0=in0, scalar=scalar, in1=in1, op0=op0, op1=op1),
            reads=rd, writes=[out])

    def act(self, out, in_, func, bias=0.0, scale=1.0, accum_out=None):
        rd = [in_] + [x for x in (bias, scale) if self._isap(x)]
        wr = [out] + ([accum_out] if accum_out is not None else [])
        self.emit("scalar", lambda e: e.activation(
            out=out, in_=in_, func=func, bias=bias, scale=scale,
            accum_out=accum_out), reads=rd, writes=wr)

    def red(self, out, in_, op=ALU.add, eng="vector"):
        self.emit(eng,
                  lambda e: e.tensor_reduce(out=out, in_=in_, axis=AX.X, op=op),
                  reads=[in_], writes=[out])

    def scan(self, out, d0, d1, initial):
        self.emit("vector", lambda e: e.tensor_tensor_scan(
            out=out, data0=d0, data1=d1, initial=initial,
            op0=ALU.mult, op1=ALU.add), reads=[d0, d1, initial], writes=[out])

    def copy(self, out, in_, eng="vector"):
        self.emit(eng, lambda e: e.tensor_copy(out, in_),
                  reads=[in_], writes=[out])

    def memset(self, ap, v, eng="vector"):
        self.emit(eng, lambda e: e.memset(ap, v), writes=[ap])

    def recip(self, out, in_, eng="vector"):
        self.emit(eng, lambda e: e.reciprocal(out, in_),
                  reads=[in_], writes=[out])

    def sb(self, ex, name, shape, dtype):
        base = self.nc.sbuf_base
        h = ex.enter_context(self.nc.sbuf_tensor(_nm(name), shape, dtype))
        self.addr[h.name] = base
        self.rowel[h.name] = int(np.prod(shape[1:])) if len(shape) > 1 else 1
        rem = self.nc.sbuf_base % 512
        if rem:
            ex.enter_context(self.nc.sbuf_tensor(
                _nm("pad"), [1, (512 - rem) // 2], BF16))
        return h

    def ps(self, ex, name, shape, dtype):
        base = self.nc.psum_base * 2048
        h = ex.enter_context(self.nc.psum_tensor(_nm(name), shape, dtype))
        self.addr[h.name] = base
        self.rowel[h.name] = int(np.prod(shape[1:])) if len(shape) > 1 else 1
        return h


def sl(t, w):
    return slice(t * w, (t + 1) * w)


def ln_tile(p, c, t, zhatT_dest, half_res=False, f32_scr=False):
    xs = c.x[:, sl(t, D)]
    p.red(c.sumc[:], xs)
    p.act(c.scr[:], xs, AF.Square, accum_out=c.sqc[:])
    p.ts(c.mc[:], c.sumc[:], 1.0 / D, ALU.mult)
    p.ts(c.msq[:], c.mc[:], c.mc[:], ALU.mult)
    p.stt(c.varc[:], c.sqc[:], 1.0 / D, c.msq[:], ALU.mult, ALU.subtract)
    p.act(c.lnc[:], c.varc[:], AF.Ln, bias=c.epscol[:])
    p.act(c.rstdc[:], c.lnc[:], AF.Exp, scale=-0.5)
    p.ts(c.zhat[:], xs, c.mc[:], ALU.subtract, s2=c.rstdc[:], op1=ALU.mult)
    if f32_scr:
        p.ts(c.scr[:], xs, c.mc[:], ALU.subtract, s2=c.rstdc[:], op1=ALU.mult)
    if half_res:
        p.stt(xs, c.zhat[:], 0.5, xs, ALU.mult, ALU.add)
    for ci in range(NC):
        p.tr(c.pm_trb[:, sl(ci, 128)], c.zhat[:, sl(ci, 128)], c.ident_b[:])
    p.act(zhatT_dest, c.pm_trb[:], AF.Copy)


def emit_moe(p, c, st):
    nc = p.nc
    with ExitStack() as ex:
        zT = p.sb(ex, "moe_zT", [128, NC * S], BF16)
        gate = p.sb(ex, "moe_gate", [128, NC * 8], F32)
        cw = p.sb(ex, "moe_cw", [128, NT * 8], F32)
        g8 = p.sb(ex, "moe_g8", [128, 8], F32)
        gm = p.sb(ex, "moe_gm", [128, 6], F32)
        w1 = p.sb(ex, "moe_w1", [128, NC * DHID], BF16)
        w2 = p.sb(ex, "moe_w2", [128, 16 * D], BF16)
        h = p.sb(ex, "moe_h", [128, 16 * 512], BF16)
        b1n = p.sb(ex, "moe_b1n", [16, 128], F32)
        b1c = p.sb(ex, "moe_b1c", [128, 16], F32)
        b2s = p.sb(ex, "moe_b2s", [8, D], F32)
        cwT = p.sb(ex, "moe_cwT", [8, 128], F32)
        zTf = p.sb(ex, "moe_zTf", [128, D], F32)
        pm = [p.ps(ex, "moe_pm", [128, 512], F32) for _ in range(2)]
        pmo = [p.ps(ex, "moe_pmo", [128, 512], F32) for _ in range(2)]
        pms = p.ps(ex, "moe_pms", [128, 128], F32)
        pmtf = p.ps(ex, "moe_pmtf", [128, D], F32)

        p.dma(gate.rearrange("p (cc e) -> p cc e", e=8),
              st.gate_W.rearrange("(cc p) e -> p cc e", p=128))
        p.dma(b2s[:], st.ffn_b2)
        p.dma(w1.rearrange("p (cc hh) -> p cc hh", cc=NC),
              st.ffn_W1[0].rearrange("(cc p) hh -> p cc hh", p=128),
              eng="gpsimd")
        p.dma(w2.rearrange("p (hc o) -> p hc o", o=D),
              st.ffn_W2[0].rearrange("(hc p) o -> p hc o", p=128),
              eng="gpsimd")
        p.dma(b1n[:], st.ffn_b1[0].rearrange("(a b) -> a b", a=16))

        for t in range(NT):
            ln_tile(p, c, t,
                    bass.AP(tensor=zT, offset=t * TT,
                            ap=[[NC * S, 128], [S, NC], [1, TT]]),
                    half_res=True, f32_scr=True)
            for ci in range(NC):
                p.tr(pmtf[:, sl(ci, 128)], c.scr[:, sl(ci, 128)], c.ident_f[:])
            p.act(zTf[:], pmtf[:], AF.Copy)
            for fc in range(NC):
                p.mm(pms[:, 0:8], zTf[:, sl(fc, 128)],
                     gate[:, sl(fc, 8)], start=(fc == 0), stop=(fc == NC - 1))
            p.red(gm[:, 0:1], pms[:, 0:8], op=ALU.max)
            p.ts(g8[:], pms[:, 0:8], gm[:, 0:1], ALU.subtract)
            p.act(g8[:], g8[:], AF.Exp)
            p.red(gm[:, 1:2], g8[:])
            p.recip(gm[:, 2:3], gm[:, 1:2])
            p.ts(g8[:], g8[:], gm[:, 2:3], ALU.mult)
            p.red(gm[:, 0:1], g8[:], op=ALU.max)
            p.ts(c.scr[:, 0:8], g8[:], gm[:, 0:1], ALU.is_ge)
            p.stt(c.scr[:, 8:16], c.scr[:, 0:8], NEG, g8[:], ALU.mult, ALU.add)
            p.red(gm[:, 1:2], c.scr[:, 8:16], op=ALU.max)
            p.tt(gm[:, 3:4], gm[:, 0:1], gm[:, 1:2], op=ALU.add)
            p.ts(gm[:, 3:4], gm[:, 3:4], 1.0e-8, ALU.add)
            p.recip(gm[:, 4:5], gm[:, 3:4])
            p.ts(c.scr[:, 0:8], g8[:], gm[:, 1:2], ALU.is_ge)
            p.tt(c.scr[:, 8:16], g8[:], c.scr[:, 0:8], op=ALU.mult)
            p.ts(cw[:, sl(t, 8)], c.scr[:, 8:16], gm[:, 4:5], ALU.mult,
                 s2=0.5, op1=ALU.mult)

        def dma_w1(e):
            p.dma(w1.rearrange("p (cc hh) -> p cc hh", cc=NC),
                  st.ffn_W1[e].rearrange("(cc p) hh -> p cc hh", p=128),
                  eng="gpsimd")

        def dma_w2(e):
            p.dma(w2.rearrange("p (hc o) -> p hc o", o=D),
                  st.ffn_W2[e].rearrange("(hc p) o -> p hc o", p=128),
                  eng="gpsimd")

        def dma_b1(e):
            p.dma(b1n[:], st.ffn_b1[e].rearrange("(a b) -> a b", a=16))

        for e_i in range(NE):
            p.tr(pms[0:128, 0:16], b1n[:], c.ident_f[0:16, 0:16])
            p.act(b1c[:], pms[:, 0:16], AF.Copy)
            for stile in range(4):
                for hc in range(16):
                    pmb_ = pm[hc % 2]
                    for fc in range(NC):
                        p.mm(pmb_[:], w1[:, fc * DHID + hc * 128:fc * DHID + (hc + 1) * 128],
                             zT[:, fc * S + stile * 512:fc * S + (stile + 1) * 512],
                             start=(fc == 0), stop=(fc == NC - 1))
                    p.act(h[:, sl(hc, 512)], pmb_[:], AF.Relu, bias=b1c[:, hc:hc + 1])
                if stile == 3 and e_i + 1 < NE:
                    dma_w1(e_i + 1)
                for tk in range(4):
                    t = stile * 4 + tk
                    for half in range(2):
                        pob_ = pmo[(tk * 2 + half) % 2]
                        for hc in range(16):
                            p.mm(pob_[:],
                                 h[:, hc * 512 + tk * 128:hc * 512 + (tk + 1) * 128],
                                 w2[:, hc * D + half * 512:hc * D + (half + 1) * 512],
                                 start=(hc == 0), stop=(hc == 15))
                        xsl = c.x[:, t * D + half * 512:t * D + (half + 1) * 512]
                        p.stt(xsl, pob_[:], cw[:, t * 8 + e_i:t * 8 + e_i + 1], xsl)
            if e_i + 1 < NE:
                dma_w2(e_i + 1)
                dma_b1(e_i + 1)

        for t in range(NT):
            p.tr(pms[0:8, 0:128], cw[:, sl(t, 8)], c.ident_f[:])
            p.act(cwT[:], pms[0:8, 0:128], AF.Copy)
            for half in range(2):
                p.mm(pmo[half][:], cwT[:], b2s[:, sl(half, 512)])
                xsl = c.x[:, t * D + half * 512:t * D + (half + 1) * 512]
                p.tt(xsl, xsl, pmo[half][:], op=ALU.add)
        p.flush()


def emit_mamba(p, c, st):
    nc = p.nc
    with ExitStack() as ex:
        zhT = p.sb(ex, "mb_zhT", [128, D], BF16)
        wbt = p.sb(ex, "mb_wbt", [128, NC * 16], BF16)
        wct = p.sb(ex, "mb_wct", [128, NC * 16], BF16)
        wdt = p.sb(ex, "mb_wdt", [128, NC], BF16)
        wnat = p.sb(ex, "mb_wnat", [16, D], F32)
        adl = p.sb(ex, "mb_adl", [128, 128], F32)
        iadl = p.sb(ex, "mb_iadl", [128, 128], F32)
        state = p.sb(ex, "mb_state", [128, 128], F32)
        bmt = p.sb(ex, "mb_bmt", [128, 16], F32)
        cmt = p.sb(ex, "mb_cmt", [128, 16], F32)
        bbb = p.sb(ex, "mb_bbb", [128, 16], F32)
        bcb = p.sb(ex, "mb_bcb", [128, 16], F32)
        bdc = p.sb(ex, "mb_bdc", [128, 1], F32)
        brow = p.sb(ex, "mb_brow", [1, 16], F32)
        dsp = p.sb(ex, "mb_dsp", [128, 6], F32)
        drow = p.sb(ex, "mb_drow", [1, 128], F32)
        dcols = p.sb(ex, "mb_dcols", [128, 128], F32)
        flat = p.sb(ex, "mb_flat", [1, 2048], F32)
        bmb = p.sb(ex, "mb_bmb", [128, 2048], F32)
        cmb = p.sb(ex, "mb_cmb", [128, 2048], F32)
        abuf = p.sb(ex, "mb_abuf", [128, 2048], F32)
        ab = p.sb(ex, "mb_ab", [128, 2048], F32)
        bw = p.sb(ex, "mb_bw", [128, 2048], F32)
        sh = p.sb(ex, "mb_sh", [128, 2048], F32)
        ybt = p.sb(ex, "mb_ybt", [128, 128], F32)
        pm = p.ps(ex, "mb_pm", [128, 512], F32)
        pms = p.ps(ex, "mb_pms", [128, 128], F32)
        pmy = p.ps(ex, "mb_pmy", [128, 128], F32)

        p.dma(adl.rearrange("p (cc n) -> p cc n", n=16),
              st.A.rearrange("(cc p) n -> p cc n", p=128))
        p.recip(iadl[:], adl[:])
        p.memset(state[:], 0.0)
        p.dma(wnat[:], st.W_B)
        for ci in range(NC):
            p.tr(pms[:, 0:16], wnat[:, sl(ci, 128)], c.ident_f[0:16, 0:16])
            p.act(wbt[:, sl(ci, 16)], pms[:, 0:16], AF.Copy)
        p.dma(wnat[:], st.W_C)
        for ci in range(NC):
            p.tr(pms[:, 0:16], wnat[:, sl(ci, 128)], c.ident_f[0:16, 0:16])
            p.act(wct[:, sl(ci, 16)], pms[:, 0:16], AF.Copy)
        p.dma(wnat[0:1, :], st.W_Delta)
        for ci in range(NC):
            p.tr(pms[:, 0:1], wnat[0:1, sl(ci, 128)], c.ident_f[0:1, 0:1])
            p.act(wdt[:, ci:ci + 1], pms[:, 0:1], AF.Copy)
        p.dma(brow[:], st.b_B.rearrange("(a n) -> a n", a=1))
        p.mm(pms[:, 0:16], c.ones_f[0:1, :], brow[:])
        p.act(bbb[:], pms[:, 0:16], AF.Copy)
        p.dma(brow[:], st.b_C.rearrange("(a n) -> a n", a=1))
        p.mm(pms[:, 0:16], c.ones_f[0:1, :], brow[:])
        p.act(bcb[:], pms[:, 0:16], AF.Copy)
        p.dma(brow[0:1, 0:1], st.b_Delta.rearrange("(a n) -> a n", a=1))
        p.dma(brow[0:1, 1:2], st.bias_Delta.rearrange("(a n) -> a n", a=1))
        p.tt(brow[0:1, 2:3], brow[0:1, 0:1], brow[0:1, 1:2], op=ALU.add)
        p.mm(pms[:, 0:1], c.ones_f[0:1, :], brow[0:1, 2:3])
        p.act(bdc[:], pms[:, 0:1], AF.Copy)

        for t in range(NT):
            ln_tile(p, c, t,
                    bass.AP(tensor=zhT, offset=0, ap=[[D, 128], [128, NC], [1, TT]]))
            for fc in range(NC):
                p.mm(pms[:, 0:16], zhT[:, sl(fc, 128)], wbt[:, sl(fc, 16)],
                     start=(fc == 0), stop=(fc == NC - 1))
            p.tt(bmt[:], pms[:, 0:16], bbb[:], op=ALU.add)
            for fc in range(NC):
                p.mm(pms[:, 0:16], zhT[:, sl(fc, 128)], wct[:, sl(fc, 16)],
                     start=(fc == 0), stop=(fc == NC - 1))
            p.tt(cmt[:], pms[:, 0:16], bcb[:], op=ALU.add)
            for fc in range(NC):
                p.mm(pms[:, 0:1], zhT[:, sl(fc, 128)], wdt[:, fc:fc + 1],
                     start=(fc == 0), stop=(fc == NC - 1))
            p.ts(dsp[:, 0:1], pms[:, 0:1], bdc[:], ALU.add)
            p.act(dsp[:, 1:2], dsp[:, 0:1], AF.Abs)
            p.act(dsp[:, 2:3], dsp[:, 1:2], AF.Exp, scale=-1.0)
            p.act(dsp[:, 3:4], dsp[:, 2:3], AF.Ln, bias=c.onecol[:])
            p.act(dsp[:, 4:5], dsp[:, 0:1], AF.Relu)
            p.tt(dsp[:, 5:6], dsp[:, 3:4], dsp[:, 4:5], op=ALU.add)
            p.tr(pms[0:1, 0:128], dsp[:, 5:6], c.ident_f[:])
            p.act(drow[:], pms[0:1, 0:128], AF.Copy)
            p.mm(pms[:], c.ones_f[0:1, :], drow[:])
            p.act(dcols[:], pms[:], AF.Copy)
            bmflat = bass.AP(tensor=st.bm_scr[0].tensor, offset=0, ap=[[2048, 1], [1, 2048]])
            cmflat = bass.AP(tensor=st.cm_scr[0].tensor, offset=0, ap=[[2048, 1], [1, 2048]])
            p.dma(st.bm_scr[0], bmt[:])
            p.dma(flat[:], bmflat)
            for k in range(4):
                p.mm(pm[:], c.ones_f[0:1, :], flat[:, sl(k, 512)])
                p.act(bmb[:, sl(k, 512)], pm[:], AF.Copy)
            p.dma(st.cm_scr[0], cmt[:])
            p.dma(flat[:], cmflat)
            for k in range(4):
                p.mm(pm[:], c.ones_f[0:1, :], flat[:, sl(k, 512)])
                p.act(cmb[:, sl(k, 512)], pm[:], AF.Copy)

            for ci in range(NC):
                v3 = [128, TT, 16]
                p.tt(abuf.rearrange("p (t n) -> p t n", n=16),
                     dcols[:, :, None].broadcast_to(v3),
                     adl[:, sl(ci, 16)][:, None, :].broadcast_to(v3))
                p.act(ab[:], abuf[:], AF.Exp)
                p.stt(bw.rearrange("p (t n) -> p t n", n=16),
                      ab.rearrange("p (t n) -> p t n", n=16), -1.0,
                      iadl[:, sl(ci, 16)][:, None, :].broadcast_to(v3),
                      ALU.add, ALU.mult)
                p.tt(bw[:], bw[:], bmb[:])
                p.tt(bw.rearrange("p (t n) -> p t n", n=16),
                     bw.rearrange("p (t n) -> p t n", n=16),
                     zhT[:, sl(ci, 128)][:, :, None].broadcast_to(v3))
                for n in range(16):
                    p.scan(bass.AP(tensor=sh, offset=n, ap=[[2048, 128], [16, TT]]),
                           bass.AP(tensor=ab, offset=n, ap=[[2048, 128], [16, TT]]),
                           bass.AP(tensor=bw, offset=n, ap=[[2048, 128], [16, TT]]),
                           state[:, ci * 16 + n:ci * 16 + n + 1])
                p.copy(state[:, sl(ci, 16)], sh[:, 2032:2048])
                p.tt(abuf[:], sh[:], cmb[:])
                p.red(ybt[:], abuf.rearrange("p (t n) -> p t n", n=16))
                p.tr(pmy[:], ybt[:], c.ident_f[:])
                xsl = c.x[:, t * D + ci * 128:t * D + (ci + 1) * 128]
                p.tt(xsl, xsl, pmy[:], op=ALU.add)
        p.flush()


def emit_l1ffn_prep(p, c, st):
    nc = p.nc
    with ExitStack() as ex:
        ldn = p.sb(ex, "lp_ldn", [4, 100], F32)
        ldT = p.sb(ex, "lp_ldT", [100, 4], F32)
        dax = p.sb(ex, "lp_dax", [100, D], F32)
        lun = p.sb(ex, "lp_lun", [50, 4], F32)
        luT = p.sb(ex, "lp_luT", [4, 50], F32)
        m1 = p.sb(ex, "lp_m1", [4, D], F32)
        m2 = p.sb(ex, "lp_m2", [50, D], F32)
        upn = p.sb(ex, "lp_upn", [128, NC * 50], F32)
        upT = p.sb(ex, "lp_upT", [50, D], F32)
        wsb = p.sb(ex, "lp_wsb", [128, NC * D], F32)
        wet = p.sb(ex, "lp_wet", [128, NC * D], BF16)
        wtt = p.sb(ex, "lp_wtt", [128, 128], F32)
        brw = p.sb(ex, "lp_brw", [1, D], F32)
        hbs = p.sb(ex, "lp_hbs", [128, D], F32)
        pm = p.ps(ex, "lp_pm", [128, 512], F32)
        pmt = p.ps(ex, "lp_pmt", [128, 128], F32)

        p.dma(ldn[:], st.lora_down)
        p.dma(dax[:], st.down_aux)
        p.dma(lun[:], st.lora_up)
        p.dma(upn.rearrange("p (cc u) -> p cc u", u=50),
              st.up_aux.rearrange("(cc p) u -> p cc u", p=128))
        p.dma(wsb.rearrange("p (co i) -> p co i", i=D),
              st.W.rearrange("(co p) i -> p co i", p=128))
        p.dma(brw[:], st.b.rearrange("(a n) -> a n", a=1))

        p.tr(pmt[0:100, 0:4], ldn[:], c.ident_f[0:4, 0:4])
        p.act(ldT[:], pmt[0:100, 0:4], AF.Copy)
        p.tr(pmt[0:4, 0:50], lun[:], c.ident_f[0:50, 0:50])
        p.act(luT[:], pmt[0:4, 0:50], AF.Copy)
        for half in range(2):
            p.mm(pm[0:4, :], ldT[:], dax[:, sl(half, 512)])
            p.act(m1[:, sl(half, 512)], pm[0:4, :], AF.Copy)
        for half in range(2):
            p.mm(pm[0:50, :], luT[:], m1[:, sl(half, 512)])
            p.act(m2[:, sl(half, 512)], pm[0:50, :], AF.Copy)
        for ci in range(NC):
            p.tr(pmt[0:50, :], upn[:, sl(ci, 50)], c.ident_f[:])
            p.act(upT[:, sl(ci, 128)], pmt[0:50, 0:128], AF.Copy)
        for ci in range(NC):
            for half in range(2):
                p.mm(pm[:], m2[:, sl(ci, 128)], upT[:, sl(half, 512)])
                for cq in range(4):
                    co = half * 4 + cq
                    p.tr(pmt[:], wsb[:, co * D + ci * 128:co * D + (ci + 1) * 128],
                         c.ident_f[:])
                    p.act(wtt[:], pmt[:], AF.Copy, scale=0.5)
                    p.stt(wet[:, ci * D + co * 128:ci * D + (co + 1) * 128],
                          pm[:, sl(cq, 128)], 0.5, wtt[:], ALU.mult, ALU.add)
        for half in range(2):
            p.mm(pm[:], c.ones_f[0:1, :], brw[:, sl(half, 512)])
            p.act(hbs[:, sl(half, 512)], pm[:], AF.Copy, scale=0.5)
        p.dma(st.wet_scr, wet[:])
        p.dma(st.hb_scr, hbs[:])
        p.flush()


def emit_l1ffn(p, c, st):
    nc = p.nc
    with ExitStack() as ex:
        zhT = p.sb(ex, "lf_zhT", [128, D], BF16)
        wet = p.sb(ex, "lf_wet", [128, NC * D], BF16)
        hbs = p.sb(ex, "lf_hbs", [128, D], F32)
        tmp = p.sb(ex, "lf_tmp", [128, 512], F32)
        pm = p.ps(ex, "lf_pm", [128, 512], F32)
        p.dma(wet[:], st.wet_scr)
        p.dma(hbs[:], st.hb_scr)
        for t in range(NT):
            ln_tile(p, c, t,
                    bass.AP(tensor=zhT, offset=0, ap=[[D, 128], [128, NC], [1, TT]]))
            for half in range(2):
                for ci in range(NC):
                    p.mm(pm[:], zhT[:, sl(ci, 128)],
                         wet[:, ci * D + half * 512:ci * D + (half + 1) * 512],
                         start=(ci == 0), stop=(ci == NC - 1))
                p.tt(tmp[:], pm[:], hbs[:, sl(half, 512)], op=ALU.add)
                xsl = c.x[:, t * D + half * 512:t * D + (half + 1) * 512]
                p.tt(xsl, xsl, tmp[:], op=ALU.add)
        p.flush()


def emit_attn(p, c, st):
    nc = p.nc
    with ExitStack() as ex:
        cqT = p.sb(ex, "at_cqT", [128, 2 * S], BF16)
        ckT = p.sb(ex, "at_ckT", [128, 2 * S], BF16)
        qrr = p.sb(ex, "at_qrr", [128, NT * 512], BF16)
        vsb = p.sb(ex, "at_vsb", [128, NT * 1040], BF16)
        krT = p.sb(ex, "at_krT", [32, S], BF16)
        wuq = p.sb(ex, "at_wuq", [128, 2 * D], BF16)
        wuk = p.sb(ex, "at_wuk", [128, 2 * D], BF16)
        wqr = p.sb(ex, "at_wqr", [128, 2 * 512], BF16)
        wo = p.sb(ex, "at_wo", [128, NC * D], BF16)

        p.dma(wuq.rearrange("p (cc f) -> p cc f", f=D),
              st.W_UQ.rearrange("(cc p) f -> p cc f", p=128), eng="gpsimd")
        p.dma(wuk.rearrange("p (cc f) -> p cc f", f=D),
              st.W_UK.rearrange("(cc p) f -> p cc f", p=128), eng="gpsimd")
        p.dma(wqr.rearrange("p (cc f) -> p cc f", f=512),
              st.W_QR.rearrange("(cc p) f -> p cc f", p=128), eng="gpsimd")
        p.dma(wo.rearrange("p (cc f) -> p cc f", f=D),
              st.W_O.rearrange("(cc p) f -> p cc f", p=128), eng="gpsimd")

        with ExitStack() as ex1:
            zhT = p.sb(ex1, "at_zhT", [128, D], BF16)
            wdq = p.sb(ex1, "at_wdq", [128, NC * DL], BF16)
            wdk = p.sb(ex1, "at_wdk", [128, NC * DL], BF16)
            wuv = p.sb(ex1, "at_wuv", [128, 2 * D], BF16)
            wkr = p.sb(ex1, "at_wkr", [128, NC * 32], BF16)
            cost = p.sb(ex1, "at_cos", [128, 16], F32)
            sint = p.sb(ex1, "at_sin", [128, 16], F32)
            rt1 = p.sb(ex1, "at_rt1", [128, 256], F32)
            rt2 = p.sb(ex1, "at_rt2", [128, 256], F32)
            krr = p.sb(ex1, "at_krr", [128, 32], BF16)
            pm = p.ps(ex1, "at_pm", [128, 512], F32)
            pmb = p.ps(ex1, "at_pmb", [128, 512], BF16)

            p.dma(wdq.rearrange("p (cc f) -> p cc f", f=DL),
                  st.W_DQ.rearrange("(cc p) f -> p cc f", p=128), eng="gpsimd")
            p.dma(wdk.rearrange("p (cc f) -> p cc f", f=DL),
                  st.W_DKV.rearrange("(cc p) f -> p cc f", p=128), eng="gpsimd")
            p.dma(wuv.rearrange("p (cc f) -> p cc f", f=D),
                  st.W_UV.rearrange("(cc p) f -> p cc f", p=128), eng="gpsimd")
            p.dma(wkr.rearrange("p (cc f) -> p cc f", f=32),
                  st.W_KR.rearrange("(cc p) f -> p cc f", p=128), eng="gpsimd")

            for t in range(NT):
                ln_tile(p, c, t,
                        bass.AP(tensor=zhT, offset=0, ap=[[D, 128], [128, NC], [1, TT]]))
                p.dma(cost[:], st.cos_t[sl(t, 128), :])
                p.dma(sint[:], st.sin_t[sl(t, 128), :])
                for lc in range(2):
                    for fc in range(NC):
                        p.mm(pm[:, 0:128],
                             wdq[:, fc * DL + lc * 128:fc * DL + (lc + 1) * 128],
                             zhT[:, sl(fc, 128)],
                             start=(fc == 0), stop=(fc == NC - 1))
                    p.act(cqT[:, lc * S + t * TT:lc * S + (t + 1) * TT],
                          pm[:, 0:128], AF.Copy)
                    for fc in range(NC):
                        p.mm(pm[:, 0:128],
                             wdk[:, fc * DL + lc * 128:fc * DL + (lc + 1) * 128],
                             zhT[:, sl(fc, 128)],
                             start=(fc == 0), stop=(fc == NC - 1))
                    p.act(ckT[:, lc * S + t * TT:lc * S + (t + 1) * TT],
                          pm[:, 0:128], AF.Copy)
                for fc in range(NC):
                    p.mm(pm[:, 0:32], zhT[:, sl(fc, 128)], wkr[:, sl(fc, 32)],
                         start=(fc == 0), stop=(fc == NC - 1))
                pe = pm[:, 0:32].rearrange("p (f two) -> p f two", two=2)
                p.tt(rt1[:, 0:16], pe[:, :, 0], cost[:])
                p.tt(rt2[:, 0:16], pe[:, :, 1], sint[:])
                p.tt(krr.rearrange("p (f two) -> p f two", two=2)[:, :, 0],
                     rt1[:, 0:16], rt2[:, 0:16], op=ALU.subtract)
                p.tt(rt1[:, 0:16], pe[:, :, 0], sint[:])
                p.tt(rt2[:, 0:16], pe[:, :, 1], cost[:])
                p.tt(krr.rearrange("p (f two) -> p f two", two=2)[:, :, 1],
                     rt1[:, 0:16], rt2[:, 0:16], op=ALU.add)
                p.tr(pmb[0:32, 0:128], krr[:], c.ident_b[:])
                p.act(krT[:, sl(t, 128)], pmb[0:32, 0:128], AF.Copy)
                for lc in range(2):
                    p.mm(pm[:], cqT[:, lc * S + t * TT:lc * S + (t + 1) * TT],
                         wqr[:, sl(lc, 512)],
                         start=(lc == 0), stop=(lc == 1))
                pe = pm.rearrange("p (h f two) -> p h f two", two=2, f=16)
                cb = cost[:, None, :].broadcast_to([128, 16, 16])
                sb = sint[:, None, :].broadcast_to([128, 16, 16])
                r3 = rt1.rearrange("p (h f) -> p h f", f=16)
                r4 = rt2.rearrange("p (h f) -> p h f", f=16)
                qv = qrr[:, sl(t, 512)].rearrange("p (h f two) -> p h f two", two=2, f=16)
                p.tt(r3, pe[:, :, :, 0], cb)
                p.tt(r4, pe[:, :, :, 1], sb)
                p.tt(qv[:, :, :, 0], r3, r4, op=ALU.subtract)
                p.tt(r3, pe[:, :, :, 0], sb)
                p.tt(r4, pe[:, :, :, 1], cb)
                p.tt(qv[:, :, :, 1], r3, r4, op=ALU.add)
                for half in range(2):
                    for lc in range(2):
                        p.mm(pm[:], ckT[:, lc * S + t * TT:lc * S + (t + 1) * TT],
                             wuv[:, lc * D + half * 512:lc * D + (half + 1) * 512],
                             start=(lc == 0), stop=(lc == 1))
                    dst = bass.AP(tensor=vsb, offset=t * 1040 + half * 520,
                                  ap=[[NT * 1040, 128], [65, 8], [1, 64]])
                    p.act(dst, pm.rearrange("p (h f) -> p h f", f=64), AF.Copy)
                p.memset(bass.AP(tensor=vsb, offset=t * 1040 + 64,
                                 ap=[[NT * 1040, 128], [65, 16]]), 1.0)
            p.flush()

        with ExitStack() as ex2:
            atT = p.sb(ex2, "at_atT", [128, S], BF16)
            qh = p.sb(ex2, "at_qh", [64, S], BF16)
            kh = p.sb(ex2, "at_kh", [64, S], BF16)
            qrh = p.sb(ex2, "at_qrh", [32, S], BF16)
            maskw = p.sb(ex2, "at_maskw", [128, 4 * 512], BF16)
            scb = [p.sb(ex2, "at_scb", [128, 512], F32) for _ in range(3)]
            etb = [p.sb(ex2, "at_etb", [128, 512], BF16) for _ in range(3)]
            rrs = p.sb(ex2, "at_rrs", [1, 512], F32)
            rbs = p.sb(ex2, "at_rbs", [64, 512], F32)
            psc = [p.ps(ex2, "at_psc", [128, 512], F32) for _ in range(3)]
            pmav = p.ps(ex2, "at_pmav", [65, 512], F32)
            pm2 = p.ps(ex2, "at_pm2", [128, 1024], F32)
            pmr = p.ps(ex2, "at_pmr", [64, 512], F32)

            p.dma(maskw[:], st.mask512)

            for h_i in range(NH):
                hc, hp = h_i // 2, h_i % 2
                for qc in range(4):
                    for lc in range(2):
                        p.mm(pm2[0:64, 0:512],
                             wuq[:, lc * D + h_i * 64:lc * D + (h_i + 1) * 64],
                             cqT[:, lc * S + qc * 512:lc * S + (qc + 1) * 512],
                             start=(lc == 0), stop=(lc == 1))
                    p.act(qh[:, sl(qc, 512)], pm2[0:64, 0:512], AF.Copy)
                    for lc in range(2):
                        p.mm(pm2[0:64, 512:1024],
                             wuk[:, lc * D + h_i * 64:lc * D + (h_i + 1) * 64],
                             ckT[:, lc * S + qc * 512:lc * S + (qc + 1) * 512],
                             start=(lc == 0), stop=(lc == 1))
                    p.act(kh[:, sl(qc, 512)], pm2[0:64, 512:1024], AF.Copy)
                for qc in range(4):
                    for tk in range(4):
                        t = qc * 4 + tk
                        p.tr(c.pm_trb[0:32, sl(tk, 128)],
                             qrr[:, t * 512 + h_i * 32:t * 512 + (h_i + 1) * 32],
                             c.ident_b[:])
                    p.act(qrh[:, sl(qc, 512)], c.pm_trb[0:32, 0:512], AF.Copy)

                pairs = [(qs, kt) for qs in range(4) for kt in range(4 * qs + 4)]

                def e_score(j):
                    qs, kt = pairs[j]
                    b = j % 3
                    qsl = slice(qs * 512, (qs + 1) * 512)
                    p.mm(psc[b][:], kh[:, sl(kt, 128)], qh[:, qsl],
                         start=True, stop=False)
                    p.mm(psc[b][:], krT[:, sl(kt, 128)], qrh[:, qsl],
                         start=False, stop=True)

                def e_sm(j):
                    qs, kt = pairs[j]
                    b = j % 3
                    p.ts(scb[b][:], psc[b][:], 80.0, ALU.min,
                         s2=-80.0, op1=ALU.max)
                    if kt >= 4 * qs:
                        r = kt - 4 * qs
                        p.tt(scb[b][:], scb[b][:],
                             maskw[:, sl(r, 512)], op=ALU.add)
                    p.act(etb[b][:], scb[b][:], AF.Exp, scale=INV_SQRT_DQK)

                def e_av(j):
                    qs, kt = pairs[j]
                    b = j % 3
                    nkt = 4 * qs + 4
                    p.mm(pmav[:],
                         vsb[:, kt * 1040 + h_i * 65:kt * 1040 + (h_i + 1) * 65],
                         etb[b][:], start=(kt == 0), stop=(kt == nkt - 1))
                    if kt == nkt - 1:
                        qsl = slice(qs * 512, (qs + 1) * 512)
                        p.recip(rrs[:], pmav[64:65, :])
                        p.mm(pmr[:], c.ones_f[0:1, 0:64], rrs[:])
                        p.act(rbs[:], pmr[:], AF.Copy)
                        p.tt(atT[hp * 64:(hp + 1) * 64, qsl],
                             pmav[0:64, :], rbs[:], op=ALU.mult)

                for j in range(len(pairs)):
                    e_score(j)
                    e_sm(j)
                    if j >= 2:
                        e_av(j - 2)
                e_av(len(pairs) - 2)
                e_av(len(pairs) - 1)
                if hp == 1:
                    for t in range(NT):
                        for half in range(2):
                            hx = half * 512
                            p.mm(pm2[:, hx:hx + 512], atT[:, sl(t, TT)],
                                 wo[:, hc * D + hx:hc * D + hx + 512])
                        for half in range(2):
                            hx = half * 512
                            xsl = c.x[:, t * D + hx:t * D + hx + 512]
                            p.tt(xsl, xsl, pm2[:, hx:hx + 512], op=ALU.add)
            p.flush()


class Ctx:
    pass


def build_nc(parts=ALL_PARTS):
    nc = bass.Bass("TRN2", target_bir_lowering=False)
    st = Ctx()
    st.x_in = nc.dram_tensor("x_in", [S, D], F32, kind="ExternalInput").ap()
    st.out = nc.dram_tensor("out", [S, D], F32, kind="ExternalOutput").ap()
    st.ident_in = nc.dram_tensor("ident_in", [128, 128], F32, kind="ExternalInput").ap()
    st.mask_in = nc.dram_tensor("mask_in", [128, 128], F32, kind="ExternalInput").ap()
    st.mask512 = nc.dram_tensor("mask512", [128, 4 * 512], BF16, kind="ExternalInput").ap()
    st.cos_t = nc.dram_tensor("cos_t", [S, 16], F32, kind="ExternalInput").ap()
    st.sin_t = nc.dram_tensor("sin_t", [S, 16], F32, kind="ExternalInput").ap()
    st.A = nc.dram_tensor("L0_attn_A", [D, 16], F32, kind="ExternalInput").ap()
    st.W_B = nc.dram_tensor("L0_attn_W_B", [16, D], F32, kind="ExternalInput").ap()
    st.b_B = nc.dram_tensor("L0_attn_b_B", [16], F32, kind="ExternalInput").ap()
    st.W_C = nc.dram_tensor("L0_attn_W_C", [16, D], F32, kind="ExternalInput").ap()
    st.b_C = nc.dram_tensor("L0_attn_b_C", [16], F32, kind="ExternalInput").ap()
    st.W_Delta = nc.dram_tensor("L0_attn_W_Delta", [1, D], F32, kind="ExternalInput").ap()
    st.b_Delta = nc.dram_tensor("L0_attn_b_Delta", [1], F32, kind="ExternalInput").ap()
    st.bias_Delta = nc.dram_tensor("L0_attn_bias_Delta", [1], F32, kind="ExternalInput").ap()
    st.gate_W = nc.dram_tensor("L0_ffn_gate_W", [D, 8], F32, kind="ExternalInput").ap()
    st.ffn_W1 = nc.dram_tensor("L0_ffn_W1", [NE, D, DHID], F32, kind="ExternalInput").ap()
    st.ffn_b1 = nc.dram_tensor("L0_ffn_b1", [NE, DHID], F32, kind="ExternalInput").ap()
    st.ffn_W2 = nc.dram_tensor("L0_ffn_W2", [NE, DHID, D], F32, kind="ExternalInput").ap()
    st.ffn_b2 = nc.dram_tensor("L0_ffn_b2", [NE, D], F32, kind="ExternalInput").ap()
    st.W_DQ = nc.dram_tensor("L1_attn_W_DQ", [D, DL], F32, kind="ExternalInput").ap()
    st.W_UQ = nc.dram_tensor("L1_attn_W_UQ", [DL, D], F32, kind="ExternalInput").ap()
    st.W_QR = nc.dram_tensor("L1_attn_W_QR", [DL, 512], F32, kind="ExternalInput").ap()
    st.W_DKV = nc.dram_tensor("L1_attn_W_DKV", [D, DL], F32, kind="ExternalInput").ap()
    st.W_UK = nc.dram_tensor("L1_attn_W_UK", [DL, D], F32, kind="ExternalInput").ap()
    st.W_UV = nc.dram_tensor("L1_attn_W_UV", [DL, D], F32, kind="ExternalInput").ap()
    st.W_KR = nc.dram_tensor("L1_attn_W_KR", [D, 32], F32, kind="ExternalInput").ap()
    st.W_O = nc.dram_tensor("L1_attn_W_O", [D, D], F32, kind="ExternalInput").ap()
    st.W = nc.dram_tensor("L1_ffn_W", [D, D], F32, kind="ExternalInput").ap()
    st.b = nc.dram_tensor("L1_ffn_b", [D], F32, kind="ExternalInput").ap()
    st.lora_down = nc.dram_tensor("L1_ffn_lora_down", [4, 100], F32, kind="ExternalInput").ap()
    st.lora_up = nc.dram_tensor("L1_ffn_lora_up", [50, 4], F32, kind="ExternalInput").ap()
    st.down_aux = nc.dram_tensor("L1_ffn_down_aux", [100, D], F32, kind="ExternalInput").ap()
    st.up_aux = nc.dram_tensor("L1_ffn_up_aux", [D, 50], F32, kind="ExternalInput").ap()
    st.bm_scr = [nc.dram_tensor(f"bm_scr{i}", [128, 16], F32, kind="Internal").ap()
                 for i in range(2)]
    st.cm_scr = [nc.dram_tensor(f"cm_scr{i}", [128, 16], F32, kind="Internal").ap()
                 for i in range(2)]
    st.wet_scr = nc.dram_tensor("wet_scr", [128, NC * D], BF16, kind="Internal").ap()
    st.hb_scr = nc.dram_tensor("hb_scr", [128, D], F32, kind="Internal").ap()

    c = Ctx()
    with ExitStack() as top:
        semnames = ("tensor", "vector", "scalar", "gpsimd",
                    "dsync", "dgp", "dscal", "dvec")
        sems = {k: top.enter_context(nc.semaphore("sem_" + k)) for k in semnames}
        p = Prog(nc, sems)
        c.x = p.sb(top, "x_sb", [128, NT * D], F32)
        c.ident_f = p.sb(top, "ident_f", [128, 128], F32)
        c.ident_b = p.sb(top, "ident_b", [128, 128], BF16)
        c.maskc = p.sb(top, "maskc", [128, 128], F32)
        c.ones_f = p.sb(top, "ones_f", [1, 128], F32)
        c.zhat = p.sb(top, "zhat", [128, D], BF16)
        c.scr = p.sb(top, "ln_scr", [128, D], F32)
        c.sumc = p.sb(top, "ln_sum", [128, 1], F32)
        c.sqc = p.sb(top, "ln_sq", [128, 1], F32)
        c.mc = p.sb(top, "ln_m", [128, 1], F32)
        c.msq = p.sb(top, "ln_msq", [128, 1], F32)
        c.varc = p.sb(top, "ln_var", [128, 1], F32)
        c.lnc = p.sb(top, "ln_ln", [128, 1], F32)
        c.rstdc = p.sb(top, "ln_rstd", [128, 1], F32)
        c.epscol = p.sb(top, "epscol", [128, 1], F32)
        c.onecol = p.sb(top, "onecol", [128, 1], F32)
        c.pm_trb = p.ps(top, "pm_trb", [128, D], BF16)
        if True:
            p.dma(c.x.rearrange("p (n d) -> p n d", d=D),
                  st.x_in.rearrange("(n p) d -> p n d", p=128))
            p.dma(c.ident_f[:], st.ident_in)
            p.dma(c.ident_b[:], st.ident_in, eng="gpsimd")
            p.dma(c.maskc[:], st.mask_in)
            p.memset(c.ones_f[:], 1.0)
            p.memset(c.epscol[:], 1.0e-5)
            p.memset(c.onecol[:], 1.0)
            p.flush()
            for part in parts:
                if part in ("moe1", "moe2"):
                    emit_moe(p, c, st)
                elif part == "mamba":
                    emit_mamba(p, c, st)
                elif part == "ffn1":
                    emit_l1ffn_prep(p, c, st)
                    emit_l1ffn(p, c, st)
                elif part == "ffn1prep":
                    emit_l1ffn_prep(p, c, st)
                elif part == "ffn2":
                    emit_l1ffn(p, c, st)
                elif part == "attn":
                    emit_attn(p, c, st)
            p.dma(st.out.rearrange("(n p) d -> p n d", p=128),
                  c.x.rearrange("p (n d) -> p n d", d=D))
            p.flush(final=True)
    return nc


def host_consts():
    ident = np.eye(128, dtype=np.float32)
    mask = np.where(np.arange(128)[:, None] <= np.arange(128)[None, :],
                    0.0, NEG).astype(np.float32)
    inv = (10000.0 ** (-np.arange(0, 32, 2, dtype=np.float64) / 32.0))
    ang = np.arange(S, dtype=np.float64)[:, None] * inv[None, :]
    cos_t = np.cos(ang).astype(np.float32)
    sin_t = np.sin(ang).astype(np.float32)
    kk = np.arange(128)[:, None]
    qq = np.arange(512)[None, :]
    m512 = np.concatenate(
        [np.where(qq >= kk + 128 * r, 0.0, NEG) for r in range(4)],
        axis=1).astype(ml_dtypes.bfloat16)
    return {"ident_in": ident, "mask_in": mask, "cos_t": cos_t, "sin_t": sin_t,
            "mask512": m512}


def make_in_maps(inputs, xfull):
    w = {k: np.ascontiguousarray(np.asarray(v, dtype=np.float32))
         for k, v in inputs.items() if k != "L0_attn_sequence"}
    consts = host_consts()
    maps = []
    for b in range(8):
        m = {"x_in": np.ascontiguousarray(xfull[b])}
        m.update(w)
        m.update(consts)
        maps.append(m)
    return maps


def kernel(**inputs):
    xfull = np.ascontiguousarray(np.asarray(inputs["L0_attn_sequence"], np.float32))
    nc = build_nc(ALL_PARTS)
    maps = make_in_maps(inputs, xfull)
    res = run_bass_kernel_spmd(nc, maps, core_ids=list(range(8)))
    return np.stack([res.results[b]["out"] for b in range(8)], axis=0)
